# version 35
# baseline (speedup 1.0000x reference)
import contextlib
import os
import numpy as np
import ml_dtypes
import concourse.bass as bass
import concourse.mybir as mybir
from concourse.bass_utils import run_bass_kernel_spmd

F32 = mybir.dt.float32
BF16 = mybir.dt.bfloat16
AF = mybir.ActivationFunctionType
ALU = mybir.AluOpType
AXX = mybir.AxisListType.X

NCORE = 8
SEQ = 2048
D = 1024
H = 16
NCHUNK = 16
GRP = 512
NGRP = 4
CPG = 4
I_FF = 938
NFT = 8
FT_W = [128] * 7 + [42]
NUNIT = 9
CB = 512
NCB = 4

STAGE = "full"


class KB:
    ENGS = ("t", "v", "s", "g", "y")
    DMA_SEMS = ("gc", "ga", "gf0", "gf1", "gf2",
                "yc", "yx0", "yx1", "yo",
                "gx0", "gx1", "gx2", "gx3", "gx4", "gh0", "gh1", "gh2", "yp")

    def __init__(self):
        self.q = {e: [] for e in self.ENGS}
        self.led = {}
        self.waited = {}

    def cnt(self, sem):
        return self.led.get(sem, 0)

    SIM = (os.environ.get("BASS_SIM_FP64") == "1"
           or os.environ.get("KBDRAIN") == "1")
    VDRAIN_HW = True
    SDRAIN_HW = False

    def _emit(self, eng, meth, kwargs, waits=(), inc=None, drain=False):
        if drain and not self.SIM:
            drain = self.VDRAIN_HW if eng == "v" else self.SDRAIN_HW
        wl = []
        for sem, val in waits:
            if val is None or val <= 0:
                continue
            if sem in self.DMA_SEMS:
                assert val == self.led.get(sem, 0), (
                    f"dma wait not at running max: {sem} {val} "
                    f"!= {self.led.get(sem, 0)}")
            key = (eng, sem)
            if self.waited.get(key, 0) >= val:
                continue
            self.waited[key] = val
            wl.append((sem, val))
        mark = None
        incs = []
        if inc is not None:
            sem, k = inc if isinstance(inc, tuple) else (inc, 1)
            if sem in self.DMA_SEMS:
                self.led[sem] = self.led.get(sem, 0) + 1
                mark = self.led[sem]
                incs.append((sem, 16))
            else:
                self.led[sem] = self.led.get(sem, 0) + k
                mark = self.led[sem]
                incs.append((sem, k))
        self.q[eng].append((wl, drain, meth, kwargs, incs))
        return mark

    def t(self, meth, kwargs, waits=(), inc=None):
        return self._emit("t", meth, kwargs, waits, inc)

    def v(self, meth, kwargs, waits=(), inc=None, drain=False):
        return self._emit("v", meth, kwargs, waits, inc, drain)

    def s(self, meth, kwargs, waits=(), inc=None, drain=False):
        return self._emit("s", meth, kwargs, waits, inc, drain)

    def g(self, meth, kwargs, waits=(), inc=None):
        return self._emit("g", meth, kwargs, waits, inc)

    def y(self, meth, kwargs, waits=(), inc=None):
        return self._emit("y", meth, kwargs, waits, inc)

    def dummy(self, waits=(), tile="pw"):
        bt = self.barrier_tiles
        self.t("matmul", dict(out=bt[tile], lhsT=bt["ib"], rhs=bt["ib"],
                              start=True, stop=True), waits=waits)

    def barrier(self):
        bt = self.barrier_tiles
        gm0 = self.cnt("gm")
        mv = self.v("tensor_copy", dict(out=bt["v"], in_=bt["z"]),
                    waits=[("gm", gm0)], inc="v", drain=True)
        ms = self.s("copy", dict(out=bt["s"], in_=bt["z"]),
                    waits=[("gm", gm0)], inc="s", drain=True)
        mg = self.g("memset", dict(ap=bt["g"], constant=0.0), inc=("gm", 1))
        mt = self.t("matmul", dict(out=bt["pt"], lhsT=bt["ib"], rhs=bt["ib"],
                                   start=True, stop=True),
                    waits=[("v", mv), ("s", ms)], inc="t")
        comp = [("t", mt), ("v", mv), ("s", ms), ("gm", mg)]
        dma = [(s_, self.cnt(s_)) for s_ in self.DMA_SEMS]
        for e in ("g", "y"):
            self._emit(e, "nop", dict(), waits=comp)
        for e in ("t", "v", "s"):
            self._emit(e, "nop", dict(), waits=comp + dma)

    def check(self):
        for (eng, sem), val in self.waited.items():
            assert val <= self.led.get(sem, 0), (
                f"hang: {eng} waits {sem}>={val} ledger={self.led.get(sem, 0)}")
            if sem in self.DMA_SEMS:
                assert eng not in ("g", "y") or sem in ("yo", "yp"), (eng, sem)


def _replay_all(nc, K, sems):
    resolved = {}
    actual = {}
    legs_box = [0]

    def cb(ins):
        if type(ins).__name__ == "InstDMACopy":
            legs_box[0] += 1

    nc.push_inst_callback(cb)

    def run(engine, items, resolve_dma):
        for waits, drain, meth, kwargs, incs in items:
            for sem, val in waits:
                if sem in KB.DMA_SEMS:
                    if not resolve_dma:
                        raise AssertionError(f"unresolvable dma wait {sem}")
                    engine.wait_ge(sems[sem], resolved[(sem, val)])
                else:
                    engine.wait_ge(sems[sem], val)
            if drain:
                engine.drain()
            legs_box[0] = 0
            ins = getattr(engine, meth)(**kwargs)
            for sem, k in incs:
                ins.then_inc(sems[sem], k)
                if sem in KB.DMA_SEMS:
                    legs = max(legs_box[0], 1)
                    actual[sem] = actual.get(sem, 0) + 16 * legs
                    resolved[(sem, K_ids[sem])] = actual[sem]
                    K_ids[sem] += 1

    K_ids = {s: 1 for s in KB.DMA_SEMS}

    with nc.Block() as block:
        @block.gpsimd
        def _(G):
            run(G, K.q["g"], resolve_dma=False)

        @block.sync
        def _(Y):
            run(Y, K.q["y"], resolve_dma=True)

        @block.tensor
        def _(T):
            run(T, K.q["t"], resolve_dma=True)

        @block.vector
        def _(V):
            run(V, K.q["v"], resolve_dma=True)

        @block.scalar
        def _(S):
            run(S, K.q["s"], resolve_dma=True)


def _tp(row, col):
    return (row, col) if (row or col) else None


def build_nc():
    nc = bass.Bass("TRN2", target_bir_lowering=False, debug=False)

    xin = nc.dram_tensor("xin", [2, SEQ, D], F32, kind="ExternalInput").ap()
    out = nc.dram_tensor("out", [2, SEQ, D], F32, kind="ExternalOutput").ap()
    wq_h = nc.dram_tensor("wq", [128, 8, D], F32, kind="ExternalInput").ap()
    wk_h = nc.dram_tensor("wk", [128, 8, D], F32, kind="ExternalInput").ap()
    wv_h = nc.dram_tensor("wv", [128, 8, D], F32, kind="ExternalInput").ap()
    wo_h = nc.dram_tensor("wo", [128, 8, D], F32, kind="ExternalInput").ap()
    bq_h = nc.dram_tensor("bq", [D], F32, kind="ExternalInput").ap()
    bk_h = nc.dram_tensor("bk", [D], F32, kind="ExternalInput").ap()
    bv_h = nc.dram_tensor("bv", [D], F32, kind="ExternalInput").ap()
    bo_h = nc.dram_tensor("bo", [D], F32, kind="ExternalInput").ap()
    gwt_h = nc.dram_tensor("gwt", [D, 8], F32, kind="ExternalInput").ap()
    gug_h = nc.dram_tensor("gug", [NUNIT, 128, 8, I_FF], F32,
                           kind="ExternalInput").ap()
    guu_h = nc.dram_tensor("guu", [NUNIT, 128, 8, I_FF], F32,
                           kind="ExternalInput").ap()
    dnt_h = nc.dram_tensor("dnt", [NUNIT, 128, 7, D], F32,
                           kind="ExternalInput").ap()
    dnt_t = nc.dram_tensor("dntt", [NUNIT, 42, D], F32,
                           kind="ExternalInput").ap()
    cmask_h = nc.dram_tensor("cmask", [128, 128], F32, kind="ExternalInput").ap()
    identf_h = nc.dram_tensor("identf", [128, 128], F32, kind="ExternalInput").ap()
    identb_h = nc.dram_tensor("identb", [128, 128], BF16, kind="ExternalInput").ap()
    onespk_h = nc.dram_tensor("onespk", [128, 128], BF16, kind="ExternalInput").ap()

    TOTAL = 212700
    _before = nc._sbuf_addr_for_side("left")
    nc.alloc_sbuf_tensor("reserved_arena", [128, TOTAL], mybir.dt.int8)
    _after = nc._sbuf_addr_for_side("left")
    base = _after - TOTAL
    cur = [base]

    def alloc(name, shape, dtype, at=None):
        nbytes = int(np.prod(shape[1:])) * mybir.dt.size(dtype)
        if at is None:
            off = (cur[0] + 31) & ~31
            cur[0] = off + nbytes
        else:
            off = at
        return nc.alloc_sbuf_tensor_at(name, list(shape), dtype, offset=off).ap()

    x1_fm = alloc("x1_fm", [128, 8, SEQ], BF16)

    arena_base = (cur[0] + 31) & ~31
    GUSL = 8 * I_FF * 2
    DTSL = 8 * D * 2
    gu_off = [arena_base, arena_base + GUSL, arena_base + 2 * GUSL + DTSL]
    gu_sl = [alloc(f"gusl{i}", [128, 8, I_FF], BF16, at=gu_off[i])
             for i in range(3)]
    dt_off = [arena_base + 2 * GUSL, arena_base + 3 * GUSL + DTSL]
    dt_sl = [alloc(f"dtsl{i}", [128, 8, D], BF16, at=dt_off[i])
             for i in range(2)]
    arena_end = arena_base + 3 * GUSL + 2 * DTSL
    wq_sb = alloc("wq_sb", [128, 8, D], BF16, at=arena_base + 0 * 16384)
    wk_sb = alloc("wk_sb", [128, 8, D], BF16, at=arena_base + 1 * 16384)
    wv_sb = alloc("wv_sb", [128, 8, D], BF16, at=arena_base + 2 * 16384)
    wo_sb = alloc("wo_sb", [128, 8, D], BF16, at=arena_base + 3 * 16384)
    cur[0] = arena_end

    r1_base = (cur[0] + 31) & ~31
    ACC = alloc("ACC", [128, 8, SEQ], F32, at=r1_base)
    pa = [r1_base]

    def palloc(name, shape, dtype):
        nbytes = int(np.prod(shape[1:])) * mybir.dt.size(dtype)
        off = (pa[0] + 31) & ~31
        pa[0] = off + nbytes
        return nc.alloc_sbuf_tensor_at(name, list(shape), dtype, offset=off).ap()

    x_fm = palloc("x_fm", [128, 8, GRP], BF16)
    q_fm = palloc("q_fm", [128, 8, GRP], BF16)
    k_fm = palloc("k_fm", [128, 8, GRP], BF16)
    k_tm = palloc("k_tm", [128, 2, D], BF16)
    v_tm = palloc("v_tm", [128, CPG, H * 65], BF16)
    x_res = palloc("x_res", [128, 5, D], BF16)
    r1_sb = palloc("r1_sb", [128, D], F32)
    x1_tmb = palloc("x1_tmb", [128, D], BF16)
    A_mT = palloc("A_mT", [128, 16, 128], BF16)
    val_tm = palloc("val_tm", [128, D], BF16)
    val_fm = palloc("val_fm", [128, 8, 128], BF16)
    S_f32 = palloc("S_f32", [128, 8, 65], F32)
    S_bf = palloc("S_bf", [128, 8, 65], BF16)
    den_t = palloc("den_t", [128, 16], F32)
    r_den = palloc("r_den", [128, 16], F32)
    assert pa[0] - r1_base <= 8 * SEQ * 4, f"R1 overflow {pa[0] - r1_base}"
    cur[0] = r1_base + 8 * SEQ * 4

    act_sb = alloc("act_sb", [128, 8, CB], BF16)
    cwb_sb = alloc("cwb_sb", [128, 2, CB], BF16)
    cw_all = alloc("cw_all", [128, 16, 8], F32)
    cwt_sb = alloc("cwt_sb", [128, 4, 128], BF16)
    onespk = alloc("onespk", [128, 128], BF16)
    sq_sb = alloc("sq_sb", [128, D], F32)
    out_sb = alloc("out_sb", [128, D], F32)
    p_sb = alloc("p_sb", [128, GRP], BF16)
    n_sb = alloc("n_sb", [128, GRP], BF16)
    en_sb = alloc("en_sb", [128, GRP], BF16)
    sg_sb = alloc("sg_sb", [128, 2, CB], BF16)
    uu_sb = alloc("uu_sb", [128, 2, CB], BF16)
    stats = alloc("stats", [128, 12], F32)
    statsb = alloc("statsb", [128, 4], F32)
    g4 = alloc("g4", [128, 152], F32)
    e_sb = alloc("e_sb", [128, 8], F32)
    eq1_sb = alloc("eq1_sb", [128, 8], F32)
    eq2_sb = alloc("eq2_sb", [128, 8], F32)
    em_sb = alloc("em_sb", [128, 8], F32)
    cw_sb = alloc("cw_sb", [128, 8], F32)
    gt_sb = alloc("gt_sb", [128, 8], F32)
    bq_sb = alloc("bq_sb", [128, 8], F32)
    bk_sb = alloc("bk_sb", [128, 8], F32)
    bv_sb = alloc("bv_sb", [128, 8], F32)
    bo_sb = alloc("bo_sb", [128, 8], F32)
    bvb = alloc("bvb", [128, D], BF16)
    bkb = alloc("bkb", [128, D], BF16)
    bob = alloc("bob", [128, D], BF16)
    bbt = alloc("bbt", [128, 128], BF16)
    gwt_sb = alloc("gwt_sb", [128, 8, 8], BF16)
    cmask = alloc("cmask", [128, 128], F32)
    identf = alloc("identf", [128, 128], F32)
    identb = alloc("identb", [128, 128], BF16)
    cvals = alloc("cvals", [128, 8], F32)
    nc.const_aps.register(F32, 1e-5, cvals[:, 0:1])
    nc.const_aps.register(F32, -0.5, cvals[:, 1:2])
    nc.const_aps.register(F32, 0.0, cvals[:, 2:3])
    nc.const_aps.register(F32, 1e-6, cvals[:, 6:7])
    assert cur[0] - base <= TOTAL, f"SBUF overflow {cur[0] - base}"

    pb = [nc.alloc_psum_tensor(f"pb{i}", [128, 512], F32) for i in range(8)]

    K = KB()
    K.barrier_tiles = {"v": cvals[:, 3:4], "s": cvals[:, 4:5],
                       "g": cvals[:, 5:6], "z": cvals[:, 2:3],
                       "ib": identb[:, 0:16],
                       "pt": pb[0][0:16, 424:440],
                       "pw": pb[7][0:16, 496:512]}

    for dst, src in ((cmask, cmask_h), (identf, identf_h), (identb, identb_h),
                     (onespk, onespk_h)):
        K.y("dma_start", dict(out=dst[:, :], in_=src[:, :]), inc=("yc", 16))
    K.y("dma_start", dict(out=bq_sb[:, :],
                          in_=bq_h.rearrange("(g p) -> p g", p=128),
                          allow_slow_non_contiguous=True), inc=("yc", 16))
    K.y("dma_start", dict(out=bk_sb[:, :],
                          in_=bk_h.rearrange("(g p) -> p g", p=128),
                          allow_slow_non_contiguous=True), inc=("yc", 16))
    K.y("dma_start", dict(out=bv_sb[:, :],
                          in_=bv_h.rearrange("(g p) -> p g", p=128),
                          allow_slow_non_contiguous=True), inc=("yc", 16))
    K.y("dma_start", dict(out=bo_sb[:, :],
                          in_=bo_h.rearrange("(g p) -> p g", p=128),
                          allow_slow_non_contiguous=True), inc=("yc", 16))
    K.g("dma_start", dict(out=gwt_sb[:, :, :],
                          in_=gwt_h.rearrange("(kc p) e -> p kc e", p=128)),
        inc=("gc", 16))
    K.g("memset", dict(ap=cvals[:, 0:1], constant=1e-5), inc=("gm", 1))
    K.g("memset", dict(ap=cvals[:, 1:2], constant=-0.5), inc=("gm", 1))
    K.g("memset", dict(ap=cvals[:, 2:3], constant=0.0), inc=("gm", 1))
    K.g("memset", dict(ap=cvals[:, 6:7], constant=1e-6), inc=("gm", 1))

    m_bias_chain = {}
    bidx = [0]

    def build_bias_tiles():
        for b_sb_, dstb in ((bk_sb, bkb), (bv_sb, bvb), (bo_sb, bob)):
            for gcol in range(8):
                i = bidx[0]
                bidx[0] += 1
                mts = K.v("tensor_single_scalar",
                          dict(out=bbt[:, :], in_=onespk[:, :],
                               scalar=b_sb_[:, gcol:gcol + 1], op=ALU.mult),
                          waits=[("yc", K.cnt("yc")),
                                 ("t", m_bias_chain.get(i - 1, 0))],
                          inc="v", drain=True)
                mmm = K.t("matmul",
                          dict(out=pb[6][:, 0:128], lhsT=bbt[:, :],
                               rhs=identb[:, :], start=True, stop=True),
                          waits=[("v", mts),
                                 ("s", m_bias_chain.get((i - 1, "s"), 0))],
                          inc="t")
                m_bias_chain[i] = mmm
                mcp = K.s("copy",
                          dict(out=dstb[:, gcol * 128:(gcol + 1) * 128],
                               in_=pb[6][:, 0:128]),
                          waits=[("t", mmm)], inc="s")
                m_bias_chain[(i, "s")] = mcp
        st["m_biasb"] = K.cnt("s")
        st["m_pb6_s"] = K.cnt("s")

    st = {
        "arena_free": [0, 0, 0, 0, 0],
        "m_biasb": 0,
        "m_gate_v": 0,
        "m_Aevac": [0, 0], "m_sqread": 0,
        "x1e_par": [0, 0], "m_projv": 0, "m_x1t_mm": 0,
        "m_xfm_last": 0, "m_valfm_last": 0,
        "attnw_mark": 0,
        "ffnw_mark": {}, "ffnw_g": {},
        "mr1": 0, "mV2": 0, "mscopy": 0, "mstate": 0,
        "mx1evac": 0, "mWo": 0, "mnorm": 0,
        "mvA": {}, "m_dagrp": {},
        "xres_free": [(None, 0)] * 5,
        "mqk_last_read": 0,
        "m_x1fm_free": 0, "m_cwfm_free": 0,
        "m_LN2bank": {}, "m_ln2sq_v": 0,
        "m_ln2_s": [0, 0], "m_ln2_v": [0, 0], "m_out_half": [0, 0],
        "m_out_store": 0,
        "m_pb6_v": 0, "m_pb7_v": 0, "m_pb6_s": 0, "m_pb7_s": 0,
        "m_exp": 0, "m_tt": 0,
        "mV3_last": 0, "m_ktm_last": 0, "m_Aevac_last": [0, 0],
        "m_xb": [0] * 5, "m_cwt_mm": 0, "mr1h": 0, "m_ln1red": 0,
        "mts4": None,
        "x1e4": [[0] * 4, [0] * 4],
    }

    def load_attn_weights(first):
        af = st["arena_free"]
        order = ((wq_h, wq_sb, (0, 1)), (wo_h, wo_sb, (2, 4)),
                 (wk_h, wk_sb, (1, 3)), (wv_h, wv_sb, (3, 2)))
        for hbm, sb, idx in order:
            waits = [] if first else [("t", max(af[i] for i in idx))]
            K.g("dma_start",
                dict(out=sb[:, :, :], in_=hbm[:, :, :]),
                waits=waits, inc=("ga", 16))
        st["attnw_mark"] = K.cnt("ga")

    def load_unit_weights(e, gen_i):
        sm = f"gf{gen_i % 3}"
        K.g("dma_start",
            dict(out=gu_sl[(2 * e) % 3][:, :, :], in_=gug_h[e]),
            waits=[("t", st["arena_free"][(2 * e) % 3])], inc=(sm, 16))
        K.g("dma_start",
            dict(out=dt_sl[e % 2][:, 0:7, :], in_=dnt_h[e]),
            waits=[("t", st["arena_free"][3 + e % 2])], inc=(sm, 16))
        K.g("dma_start", dict(out=dt_sl[e % 2][0:42, 7, :],
                              in_=dnt_t[e]), inc=(sm, 16))
        st["ffnw_g"][e] = (sm, K.cnt(sm))
        sm2 = f"gh{gen_i % 3}"
        K.g("dma_start",
            dict(out=gu_sl[(2 * e + 1) % 3][:, :, :], in_=guu_h[e]),
            waits=[("t", st["arena_free"][(2 * e + 1) % 3])], inc=(sm2, 16))
        st["ffnw_mark"][e] = (sm2, K.cnt(sm2))

    def elu1_evac(bank, bias_col, dst, n):
        if bias_col is not None:
            K.v("tensor_scalar", dict(out=p_sb[:, 0:n], in0=bank,
                                      scalar1=bias_col, scalar2=0.0,
                                      op0=ALU.add, op1=ALU.max), drain=True)
            mn = K.v("tensor_scalar", dict(out=n_sb[:, 0:n], in0=bank,
                                           scalar1=bias_col, scalar2=0.0,
                                           op0=ALU.add, op1=ALU.min),
                     waits=[("s", st["m_exp"])], inc="v")
        else:
            K.v("tensor_single_scalar", dict(out=p_sb[:, 0:n], in_=bank,
                                             scalar=0.0, op=ALU.max), drain=True)
            mn = K.v("tensor_single_scalar", dict(out=n_sb[:, 0:n], in_=bank,
                                                  scalar=0.0, op=ALU.min),
                     waits=[("s", st["m_exp"])], inc="v")
        me = K.s("activation", dict(out=en_sb[:, 0:n], in_=n_sb[:, 0:n],
                                    func=AF.Exp),
                 waits=[("v", mn), ("v", st["m_tt"])], inc="s")
        st["m_exp"] = me
        mq = K.v("tensor_tensor", dict(out=dst, in0=p_sb[:, 0:n],
                                       in1=en_sb[:, 0:n], op=ALU.add),
                 waits=[("s", me), ("t", st["mqk_last_read"])],
                 inc="v", drain=True)
        st["m_tt"] = mq
        return mn, mq

    def phase_A(s):
        K.barrier()
        v_ones = v_tm.rearrange("p c (h m) -> p c h m", m=65)[:, :, :, 64:65]
        K.g("memset", dict(ap=v_ones, constant=1.0), inc=("gm", 1))
        m_ones = K.cnt("gm")
        pend = {}
        apend = {}
        pend2 = {}

        def pbase(c):
            return 2 if c % 2 == 0 else 4

        def num_slice(h_, c):
            if h_ < 7:
                return pb[6][:, h_ * 65:h_ * 65 + 65]
            if h_ < 14:
                return pb[7][:, (h_ - 7) * 65:(h_ - 7) * 65 + 65]
            off = 440 if c % 2 == 0 else 375
            return pb[h_ - 14][:, off:off + 65]

        def stage1_A(c):
            cc = c % CPG
            B = pbase(c)
            p = {}
            mwave = {}
            for w8 in range(2):
                for hh in range(8):
                    h = 8 * w8 + hh
                    b = (h % 2) * 64
                    pair = h // 2
                    Ab = pb[B + h % 2][:, ((h // 2) % 4) * 128:
                                      ((h // 2) % 4) * 128 + 128]
                    if w8 == 0:
                        bk4 = (h % 2) * 2 + ((h // 2) % 4) // 2
                        w = [("v", st["m_tt"]), ("gm", m_ones),
                             ("s", st["x1e4"][c % 2][bk4])]
                    else:
                        w = [("v", st["m_tt"]), ("gm", m_ones),
                             ("v", mwave.get((w8 - 1, h % 2), 0))]
                    K.t("matmul",
                        dict(out=Ab,
                             lhsT=k_fm[b:b + 64, pair,
                                       cc * 128:(cc + 1) * 128],
                             rhs=q_fm[b:b + 64, pair,
                                      cc * 128:(cc + 1) * 128],
                             start=True, stop=True,
                             tile_position=_tp(b, 0)),
                        waits=w, inc=("t" if hh >= 6 else None))
                    if hh >= 6:
                        mwave[("t", w8, h % 2)] = K.cnt("t")
                for par in range(2):
                    mwave[(w8, par)] = K.v(
                        "tensor_tensor",
                        dict(out=A_mT[:, 8 * par + 4 * w8:
                                      8 * par + 4 * w8 + 4, :],
                             in0=pb[B + par][:, :].rearrange(
                                 "p (a m) -> p a m", m=128),
                             in1=cmask.unsqueeze(1).broadcast_to(
                                 [128, 4, 128]),
                             op=ALU.mult),
                        waits=[("t", mwave[("t", w8, par)])], inc="v")
            p["mwave"] = mwave
            st["m_Aevac_last"] = [mwave[(1, 0)], mwave[(1, 1)]]
            apend[c] = p

        def stage1_kT(c):
            cc = c % CPG
            B = pbase(c)
            p = apend[c]
            mwave = p["mwave"]
            mkt = {}
            for gg in range(8):
                bank = pb[B + gg // 4][:, (gg % 4) * 128:(gg % 4) * 128 + 128]
                w = [("v", mwave[(1, 0)]), ("v", mwave[(1, 1)])]
                mm = K.t("matmul",
                         dict(out=bank,
                              lhsT=k_fm[:, gg, cc * 128:(cc + 1) * 128],
                              rhs=identb[:, :], start=True, stop=True),
                         waits=w, inc=("t" if gg % 4 == 3 else None))
                if gg % 4 == 3:
                    mkt[("t", gg // 4)] = mm
            for bk2 in range(2):
                K.s("copy",
                    dict(out=k_tm[:, c % 2, bk2 * 512:(bk2 + 1) * 512],
                         in_=pb[B + bk2][:, 0:512]),
                    waits=[("t", mkt[("t", bk2)])], inc="s")
            m_ktm = K.cnt("s")
            p["m_ktm"] = m_ktm
            st["m_ktm_last"] = m_ktm

        def stage1_rest(c):
            cc = c % CPG
            B = pbase(c)
            p = apend.pop(c)
            mwave = p["mwave"]
            m_ktm = p["m_ktm"]
            for h in range(H):
                pair = h // 2
                w = [("s", m_ktm)]
                if h % 8 == 0:
                    w.append(("s", st["mscopy"]))
                K.t("matmul",
                    dict(out=pb[h // 8][(h % 2) * 64:(h % 2) * 64 + 64,
                                        (pair % 4) * 65:
                                        (pair % 4) * 65 + 65],
                         lhsT=k_tm[:, c % 2, h * 64:(h + 1) * 64],
                         rhs=v_tm[:, cc, h * 65:(h + 1) * 65],
                         start=True, stop=True,
                         tile_position=_tp(0, (h % 2) * 64)),
                    waits=w, inc=("t" if h == 15 else None))
            st["mstate"] = K.cnt("t")
            for h in range(H):
                b = (h % 2) * 64
                pair = h // 2
                w8 = h // 8
                aslot = 8 * (h % 2) + 4 * w8 + (h // 2) % 4
                nb = num_slice(h, c)
                if h < 7:
                    w = [("v", mwave[(w8, h % 2)]), ("s", st["mr1h"]),
                         ("s", st["m_pb6_s"])]
                elif h < 14:
                    w = [("v", mwave[(w8, h % 2)]), ("s", st["mr1"]),
                         ("s", st["m_pb7_s"])]
                else:
                    w = [("v", mwave[(w8, h % 2)]), ("s", st["mr1"]),
                         ("s", st["mscopy"])]
                K.t("matmul",
                    dict(out=nb, lhsT=A_mT[:, aslot, :],
                         rhs=v_tm[:, cc, h * 65:(h + 1) * 65],
                         start=True, stop=(c == 0)),
                    waits=w,
                    inc=("t" if (c == 0 and h == 15) else None))
                if c > 0:
                    K.t("matmul",
                        dict(out=nb,
                             lhsT=q_fm[b:b + 64, pair,
                                       cc * 128:(cc + 1) * 128],
                             rhs=S_bf[b:b + 64, pair, :],
                             start=False, stop=True,
                             tile_position=_tp(b, 0)),
                        waits=[("s", st["mscopy"])],
                        inc=("t" if h == 15 else None))
            m_nend = K.cnt("t")
            p["m_Aevac"] = [mwave[(1, 0)], mwave[(1, 1)]]
            m_num_all = m_nend
            offc = 440 if c % 2 == 0 else 375
            K.v("tensor_single_scalar",
                dict(out=den_t[:, 0:7],
                     in_=pb[6][:, 0:455].rearrange(
                         "p (h m) -> p h m", m=65)[:, :, 64],
                     scalar=1e-6, op=ALU.add),
                waits=[("t", m_num_all)], drain=True)
            K.v("tensor_single_scalar",
                dict(out=den_t[:, 7:14],
                     in_=pb[7][:, 0:455].rearrange(
                         "p (h m) -> p h m", m=65)[:, :, 64],
                     scalar=1e-6, op=ALU.add), drain=True)
            K.v("tensor_single_scalar",
                dict(out=den_t[:, 14:15], in_=pb[0][:, offc + 64:offc + 65],
                     scalar=1e-6, op=ALU.add), drain=True)
            K.v("tensor_single_scalar",
                dict(out=den_t[:, 15:16], in_=pb[1][:, offc + 64:offc + 65],
                     scalar=1e-6, op=ALU.add), drain=True)
            p["mV2"] = K.v("reciprocal", dict(out=r_den[:, :],
                                              in_=den_t[:, :]),
                           inc="v", drain=True)
            K.v("tensor_tensor",
                dict(out=val_tm[:, 0:448].rearrange("p (h m) -> p h m", m=64),
                     in0=pb[6][:, 0:455].rearrange(
                         "p (h m) -> p h m", m=65)[:, :, 0:64],
                     in1=r_den[:, 0:7].unsqueeze(-1).broadcast_to(
                         [128, 7, 64]),
                     op=ALU.mult), drain=True)
            K.v("tensor_tensor",
                dict(out=val_tm[:, 448:896].rearrange("p (h m) -> p h m", m=64),
                     in0=pb[7][:, 0:455].rearrange(
                         "p (h m) -> p h m", m=65)[:, :, 0:64],
                     in1=r_den[:, 7:14].unsqueeze(-1).broadcast_to(
                         [128, 7, 64]),
                     op=ALU.mult), drain=True)
            K.v("tensor_single_scalar",
                dict(out=val_tm[:, 896:960], in_=pb[0][:, offc:offc + 64],
                     scalar=r_den[:, 14:15], op=ALU.mult), drain=True)
            p["mV3"] = K.v("tensor_single_scalar",
                           dict(out=val_tm[:, 960:1024],
                                in_=pb[1][:, offc:offc + 64],
                                scalar=r_den[:, 15:16], op=ALU.mult),
                           inc="v", drain=True)
            madd = {}
            for bk2 in range(2):
                psrc = pb[bk2][:, 0:260]
                dstf = S_f32[:, 4 * bk2:4 * bk2 + 4, :]
                if bk2 == 0:
                    K.v("nop", dict(),
                        waits=[("t", st["mstate"]), ("s", st["mscopy"])])
                if c == 0:
                    madd[bk2] = K.v("tensor_copy",
                                    dict(out=dstf, in_=psrc.rearrange(
                                        "p (a m) -> p a m", m=65)),
                                    inc="v", drain=True)
                else:
                    madd[bk2] = K.v(
                        "tensor_tensor",
                        dict(out=dstf, in0=dstf,
                             in1=psrc.rearrange("p (a m) -> p a m", m=65),
                             op=ALU.add), inc="v", drain=True)
            p["madd"] = madd
            st["mV3_last"] = p["mV3"]
            pend[c] = p
            pend2[c] = p

        def stage1_scast(c):
            p = pend2.pop(c)
            madd = p["madd"]
            for bk2 in range(2):
                K.s("copy",
                    dict(out=S_bf[:, 4 * bk2:4 * bk2 + 4, :],
                         in_=S_f32[:, 4 * bk2:4 * bk2 + 4, :]),
                    waits=[("v", madd[bk2])],
                    inc=("s" if bk2 == 1 else None), drain=True)
            st["mscopy"] = K.cnt("s")

        def stage2a(c):
            cc = c % CPG
            B = pbase(c)
            p = pend[c]
            mvT = {}
            for tt in range(8):
                bank = pb[B + tt // 4][:, (tt % 4) * 128:(tt % 4) * 128 + 128]
                w = [("v", p["mV3"]), ("v", p["m_Aevac"][0]),
                     ("v", p["m_Aevac"][1]), ("s", p["m_ktm"]),
                     ("s", st["m_xfm_last"]), ("yc", K.cnt("yc"))]
                mm = K.t("matmul",
                         dict(out=bank,
                              lhsT=val_tm[:, tt * 128:(tt + 1) * 128],
                              rhs=identb[:, :], start=True, stop=True),
                         waits=w, inc=("t" if tt % 4 == 3 else None))
                if tt % 4 == 3:
                    mvT[("t", tt // 4)] = mm
            for bk2 in range(4):
                mvT[bk2] = K.s(
                    "copy",
                    dict(out=val_fm[:, 2 * bk2:2 * bk2 + 2, :],
                         in_=pb[B + bk2 // 2][:, (bk2 % 2) * 256:
                                              (bk2 % 2) * 256 + 256].rearrange(
                             "p (a m) -> p a m", m=128)),
                    waits=[("t", mvT[("t", bk2 // 2)])],
                    inc="s")
            m_valfm = K.cnt("s")
            p["m_valfm"] = m_valfm
            st["m_valfm_last"] = m_valfm

        def stage2b(c):
            cc = c % CPG
            p = pend[c]
            m_valfm = p["m_valfm"]
            for half in range(2):
                bank = pb[6 + half]
                for mc in range(8):
                    w = []
                    if half == 0 and mc == 0:
                        w = [("s", m_valfm), ("v", p["mV3"]),
                             ("s", st["mr1"]), ("v", st["m_projv"]),
                             ("s", st["m_pb6_s"]), ("s", st["m_pb7_s"])]
                    K.t("matmul",
                        dict(out=bank[:, :], lhsT=val_fm[:, mc, :],
                             rhs=wo_sb[:, mc, half * 512:(half + 1) * 512],
                             start=(mc == 0), stop=False),
                        waits=w, inc=None)
                K.t("matmul",
                    dict(out=bank[:, :], lhsT=identb[:, :],
                         rhs=x_res[:, c % 5, half * 512:(half + 1) * 512],
                         start=False, stop=True),
                    waits=[("v", st["m_xb"][c % 5])] if half == 0 else [],
                    inc=("t" if half == 1 else None))
            st["mWo"] = K.cnt("t")
            for half in range(2):
                K.s("copy",
                    dict(out=r1_sb[:, half * 512:(half + 1) * 512],
                         in_=pb[6 + half][:, :]),
                    waits=([("t", st["mWo"]), ("s", st["mnorm"]),
                            ("v", st["m_ln1red"])] if half == 0 else []),
                    inc="s", drain=True)
                if half == 0:
                    st["mr1h"] = K.cnt("s")
            st["mr1"] = K.cnt("s")
            st["xres_free"][c % 5] = ("s", st["mr1"])
            st["m_ln1red"] = K.v("reduce_sum",
                                 dict(out=stats[:, 0:1], in_=r1_sb[:, :],
                                      axis=AXX, negate=True),
                                 waits=[("s", st["mr1"])], inc="v",
                                 drain=True)
            msq = K.s("activation", dict(out=sq_sb[:, :], in_=r1_sb[:, :],
                                         func=AF.Square),
                      waits=[("s", st["mr1"]), ("v", st["m_sqread"])],
                      inc="s", drain=True)
            st["m_sqread"] = K.v("reduce_sum",
                                 dict(out=stats[:, 2:3], in_=sq_sb[:, :],
                                      axis=AXX),
                                 waits=[("s", msq)], inc="v", drain=True)
            K.v("scalar_tensor_tensor",
                dict(out=stats[:, 4:5], in0=stats[:, 0:1],
                     scalar=1.0 / (D * D), in1=stats[:, 0:1],
                     op0=ALU.mult, op1=ALU.mult), drain=True)
            mvar = K.v("scalar_tensor_tensor",
                       dict(out=stats[:, 5:6], in0=stats[:, 2:3],
                            scalar=1.0 / D, in1=stats[:, 4:5],
                            op0=ALU.mult, op1=ALU.subtract),
                       inc="v", drain=True)
            K.s("activation", dict(out=stats[:, 6:7], in_=stats[:, 5:6],
                                   func=AF.Ln, bias=1e-5),
                waits=[("v", mvar)], drain=True)
            mrstd = K.s("activation", dict(out=stats[:, 7:8],
                                           in_=stats[:, 6:7], func=AF.Exp,
                                           scale=-0.5), inc="s", drain=True)
            mbias = K.v("scalar_tensor_tensor",
                        dict(out=statsb[:, 0:1], in0=stats[:, 0:1],
                             scalar=1.0 / D, in1=stats[:, 7:8],
                             op0=ALU.mult, op1=ALU.mult),
                        waits=[("s", mrstd)], inc="v", drain=True)
            st["mnorm"] = K.s("activation",
                              dict(out=x1_tmb[:, :], in_=r1_sb[:, :],
                                   func=AF.Identity, bias=statsb[:, 0:1],
                                   scale=stats[:, 7:8]),
                              waits=[("v", mbias),
                                     ("t", st["m_x1t_mm"])],
                              inc="s", drain=True)
            p["mnorm"] = st["mnorm"]

        def stage3(c):
            B = pbase(c)
            p = pend.pop(c)
            mx1 = {}
            for dt in range(8):
                bank = pb[B + dt // 4][:, (dt % 4) * 128:(dt % 4) * 128 + 128]
                w = [("s", p["mnorm"]), ("s", p["m_valfm"]),
                     ("s", st["m_xfm_last"])]
                mm = K.t("matmul",
                         dict(out=bank,
                              lhsT=x1_tmb[:, dt * 128:(dt + 1) * 128],
                              rhs=identb[:, :], start=True, stop=True),
                         waits=w, inc=("t" if dt % 4 == 3 else None))
                if dt % 4 == 3:
                    mx1[dt // 4] = mm
            st["m_x1t_mm"] = K.cnt("t")
            for bk2 in range(4):
                K.s("copy",
                    dict(out=x1_fm[:, 2 * bk2:2 * bk2 + 2,
                                   c * 128:(c + 1) * 128],
                         in_=pb[B + bk2 // 2][:, (bk2 % 2) * 256:
                                              (bk2 % 2) * 256 + 256].rearrange(
                             "p (a m) -> p a m", m=128)),
                    waits=[("t", mx1[bk2 // 2]), ("t", st["m_x1fm_free"])],
                    inc="s")
                st["x1e4"][c % 2][bk2] = K.cnt("s")
            st["x1e_par"][c % 2] = K.cnt("s")
            st["mx1evac"] = K.cnt("s")

        def proj(g):
            st["mqk_last_read"] = K.cnt("t")
            m_xev = {}
            xb_tm = {}
            for cc in range(CPG):
                c = CPG * g + cc
                sl = c % 5
                wsem, wval = st["xres_free"][sl]
                mload = K.g("dma_start",
                            dict(out=x_res[:, sl, :],
                                 in_=xin[s, c * 128:(c + 1) * 128, :]),
                            waits=([(wsem, wval)] if wsem else []),
                            inc=(f"gx{sl}", 16))
                for dt in range(8):
                    bank = pb[2 + dt // 2][:, (dt % 2) * 128:
                                           (dt % 2) * 128 + 128]
                    w = [(f"gx{sl}", mload), ("gm", m_ones),
                         ("s", m_xev.get(dt // 2, 0)),
                         ("s", st["x1e_par"][0]), ("s", st["x1e_par"][1]),
                         ("s", st["m_valfm_last"]), ("s", st["mr1"]),
                         ("s", st["m_ktm_last"]),
                         ("v", st["m_Aevac_last"][0]),
                         ("v", st["m_Aevac_last"][1])]
                    mT = K.t("matmul",
                             dict(out=bank,
                                  lhsT=x_res[:, sl, dt * 128:(dt + 1) * 128],
                                  rhs=identb[:, :], start=True, stop=True),
                             waits=w, inc=("t" if dt % 2 == 1 else None))
                    if dt % 2 == 1:
                        m_xev[("t", dt // 2)] = mT
                xb_tm[cc] = K.cnt("t")
                for bk2 in range(4):
                    m_xev[bk2] = K.s(
                        "copy",
                        dict(out=x_fm[:, 2 * bk2:2 * bk2 + 2,
                                      cc * 128:(cc + 1) * 128],
                             in_=pb[2 + bk2][:, 0:256].rearrange(
                                 "p (a m) -> p a m", m=128)),
                        waits=[("t", m_xev[("t", bk2)])], inc="s")
            m_xfm = K.cnt("s")
            st["m_xfm_last"] = m_xfm
            for wsb, bias_sb, dest in ((wq_sb, bq_sb, q_fm),
                                       (wk_sb, bk_sb, k_fm)):
                for dt in range(8):
                    bank = pb[6 + dt % 2]
                    for kc in range(8):
                        w = []
                        if kc == 0:
                            w = [("s", m_xfm), ("ga", st["attnw_mark"]),
                                 ("s", st["mr1"]), ("v", st["mV3_last"]),
                                 ("s", st["m_pb6_s" if dt % 2 == 0
                                          else "m_pb7_s"])]
                        mm = K.t("matmul",
                                 dict(out=bank[:, :],
                                      lhsT=wsb[:, kc, dt * 128:(dt + 1) * 128],
                                      rhs=x_fm[:, kc, :],
                                      start=(kc == 0), stop=(kc == 7)),
                                 waits=w, inc=("t" if kc == 7 else None))
                    K.s("activation",
                        dict(out=p_sb[:, :], in_=bank[:, :], func=AF.Relu,
                             bias=bias_sb[:, dt:dt + 1]),
                        waits=[("t", mm), ("v", st["m_tt"])], drain=True)
                    me = K.s("activation",
                             dict(out=en_sb[:, :], in_=bank[:, :],
                                  func=AF.Exp,
                                  bias=bias_sb[:, dt:dt + 1]),
                             inc="s", drain=True)
                    if dt % 2 == 0:
                        st["m_pb6_s"] = me
                    else:
                        st["m_pb7_s"] = me
                    st["m_tt"] = K.v("scalar_tensor_tensor",
                                     dict(out=dest[:, dt, :], in0=en_sb[:, :],
                                          scalar=1.0, in1=p_sb[:, :],
                                          op0=ALU.min, op1=ALU.add),
                                     waits=[("s", me),
                                            ("t", st["mqk_last_read"])],
                                     inc="v", drain=True)
            vpb = {0: 0, 1: 0}
            for cc in range(CPG):
                for half in range(2):
                    bank = pb[6 + half]
                    for kc in range(7):
                        w = []
                        if kc == 0:
                            w = [("s", st["m_pb6_s" if half == 0
                                         else "m_pb7_s"]),
                                 ("v", vpb[half])]
                        K.t("matmul",
                            dict(out=bank[:, :],
                                 lhsT=x_fm[:, kc, cc * 128:(cc + 1) * 128],
                                 rhs=wv_sb[:, kc, half * 512:(half + 1) * 512],
                                 start=(kc == 0), stop=False),
                            waits=w)
                    mm = K.t("matmul",
                             dict(out=bank[:, :],
                                  lhsT=x_fm[:, 7, cc * 128:(cc + 1) * 128],
                                  rhs=wv_sb[:, 7, half * 512:(half + 1) * 512],
                                  start=False, stop=True), inc="t")
                    dst = v_tm[:, cc, half * 520:(half + 1) * 520].rearrange(
                        "p (h m) -> p h m", m=65)[:, :, 0:64]
                    src = bank[:, :].rearrange("p (h m) -> p h m", m=64)
                    bsrc = bvb[:, half * 512:(half + 1) * 512].rearrange(
                        "p (h m) -> p h m", m=64)
                    vpb[half] = K.v("tensor_tensor",
                                    dict(out=dst, in0=src, in1=bsrc,
                                         op=ALU.add),
                                    waits=[("t", mm), ("t", st["mstate"]),
                                           ("s", st["m_biasb"])], inc="v")
            st["m_projv"] = K.cnt("v")
            for cc in range(CPG):
                c = CPG * g + cc
                sl = c % 5
                K.v("tensor_tensor",
                    dict(out=x_res[:, sl, :], in0=x_res[:, sl, :],
                         in1=bob[:, :], op=ALU.add),
                    waits=[("t", xb_tm[cc]), ("s", st["m_biasb"])],
                    inc="v", drain=True)
                st["m_xb"][sl] = K.cnt("v")
            if g == NGRP - 1:
                mfree = K.cnt("t")
                st["arena_free"][0] = mfree
                st["arena_free"][1] = mfree
                st["arena_free"][3] = mfree

        for g in range(NGRP):
            for cc in range(CPG):
                c = CPG * g + cc
                if c - 2 >= 0:
                    stage3(c - 2)
                if cc == 0:
                    proj(g)
                stage1_A(c)
                if c - 1 >= 0:
                    stage1_scast(c - 1)
                    stage2a(c - 1)
                stage1_kT(c)
                if c - 1 >= 0:
                    stage2b(c - 1)
                stage1_rest(c)
        stage3(NCHUNK - 2)
        stage1_scast(NCHUNK - 1)
        stage2a(NCHUNK - 1)
        stage2b(NCHUNK - 1)
        stage3(NCHUNK - 1)
        st["mqk_last_read"] = K.cnt("t")
        st["arena_free"][2] = K.cnt("t")
        st["arena_free"][4] = K.cnt("t")

    def ffn(s):
        K.barrier()
        e4 = g4[:, 0:32]
        e43 = e4.rearrange("p (c e) -> p c e", e=8)
        eqa = g4[:, 32:64]
        eqa3 = eqa.rearrange("p (c e) -> p c e", e=8)
        emb = g4[:, 64:96]
        emb3 = emb.rearrange("p (c e) -> p c e", e=8)
        gsum = g4[:, 96:100]
        gm1 = g4[:, 100:104]
        gm2 = g4[:, 104:108]
        ginv = g4[:, 108:112]
        gw1 = g4[:, 112:116]
        gw2 = g4[:, 116:120]
        gdw = g4[:, 120:124]
        m_gate_ps = 0
        for q4 in range(NCHUNK // 4):
            for j4 in range(4):
                c = 4 * q4 + j4
                for kc in range(8):
                    w = []
                    if kc == 0:
                        w = [("s", m_gate_ps), ("s", st["mx1evac"]),
                             ("gc", K.cnt("gc"))]
                    mm = K.t("matmul",
                             dict(out=pb[6][:, j4 * 8:j4 * 8 + 8],
                                  lhsT=x1_fm[:, kc, c * 128:(c + 1) * 128],
                                  rhs=gwt_sb[:, kc, :],
                                  start=(kc == 0), stop=(kc == 7)),
                             waits=w, inc=("t" if kc == 7 else None))
                K.dummy()
            me = K.s("activation", dict(out=e4, in_=pb[6][:, 0:32],
                                        func=AF.Exp),
                     waits=[("t", mm), ("v", st["m_gate_v"])],
                     inc="s", drain=True)
            m_gate_ps = me
            K.v("reduce_sum", dict(out=gsum, in_=e43, axis=AXX),
                waits=[("s", me)])
            K.v("reduce_max", dict(out=gm1, in_=e43, axis=AXX), drain=True)
            K.v("tensor_tensor",
                dict(out=eqa3, in0=e43,
                     in1=gm1.unsqueeze(-1).broadcast_to([128, 4, 8]),
                     op=ALU.is_ge), drain=True)
            K.v("scalar_tensor_tensor",
                dict(out=emb, in0=eqa, scalar=-1e30, in1=e4,
                     op0=ALU.mult, op1=ALU.add), drain=True)
            K.v("reduce_max", dict(out=gm2, in_=emb3, axis=AXX), drain=True)
            K.v("reciprocal", dict(out=ginv, in_=gsum), drain=True)
            K.v("tensor_tensor", dict(out=gw1, in0=gm1, in1=ginv,
                                      op=ALU.mult), drain=True)
            K.v("tensor_tensor", dict(out=gw2, in0=gm2, in1=ginv,
                                      op=ALU.mult), drain=True)
            K.v("scalar_tensor_tensor",
                dict(out=gdw, in0=gw2, scalar=1.0, in1=gw1,
                     op0=ALU.mult, op1=ALU.add), drain=True)
            K.v("tensor_single_scalar", dict(out=gdw, in_=gdw, scalar=1e-8,
                                             op=ALU.add), drain=True)
            K.v("reciprocal", dict(out=gdw, in_=gdw), drain=True)
            K.v("tensor_tensor", dict(out=gw1, in0=gw1, in1=gdw,
                                      op=ALU.mult), drain=True)
            K.v("tensor_tensor", dict(out=gw2, in0=gw2, in1=gdw,
                                      op=ALU.mult), drain=True)
            K.v("tensor_tensor",
                dict(out=emb3, in0=emb3,
                     in1=gm2.unsqueeze(-1).broadcast_to([128, 4, 8]),
                     op=ALU.is_ge), drain=True)
            K.v("tensor_tensor",
                dict(out=cw_all[:, 4 * q4:4 * q4 + 4, :], in0=eqa3,
                     in1=gw1.unsqueeze(-1).broadcast_to([128, 4, 8]),
                     op=ALU.mult),
                waits=[("t", st["m_cwfm_free"])], drain=True)
            K.v("tensor_tensor",
                dict(out=emb3, in0=emb3,
                     in1=gw2.unsqueeze(-1).broadcast_to([128, 4, 8]),
                     op=ALU.mult), drain=True)
            st["m_gate_v"] = K.v("tensor_tensor",
                                 dict(out=cw_all[:, 4 * q4:4 * q4 + 4, :],
                                      in0=cw_all[:, 4 * q4:4 * q4 + 4, :],
                                      in1=emb3, op=ALU.add),
                                 inc="v", drain=True)
        m_gating_done = K.cnt("v")


        def ln2_chunk(c, wextra, pi=0):
            pair = (pb[6], pb[7]) if pi == 0 else (pb[4], pb[5])
            for dtile in range(8):
                bank = pair[dtile // 4][:, (dtile % 4) * 128:
                                        (dtile % 4) * 128 + 128]
                w = []
                if dtile == 0:
                    w = wextra + [("s", st["m_ln2_s"][pi]),
                                  ("v", st["m_ln2_v"][pi])]
                    if pi == 0:
                        w += [("s", st["m_pb6_s"]), ("s", st["m_pb7_s"])]
                mT = K.t("matmul",
                         dict(out=bank,
                              lhsT=ACC[:, dtile, c * 128:(c + 1) * 128],
                              rhs=identf[:, :], start=True, stop=True),
                         waits=w, inc=("t" if dtile == 7 else None))
            K.v("reduce_sum", dict(out=stats[:, 0:1], in_=pair[0][:, :],
                                   axis=AXX, negate=True), waits=[("t", mT)])
            mred = K.v("reduce_sum", dict(out=stats[:, 1:2], in_=pair[1][:, :],
                                          axis=AXX, negate=True),
                       inc="v", drain=True)
            st["m_ln2_v"][pi] = mred
            K.v("tensor_tensor", dict(out=stats[:, 2:3], in0=stats[:, 0:1],
                                      in1=stats[:, 1:2], op=ALU.add),
                drain=True)
            K.s("activation", dict(out=sq_sb[:, 0:512], in_=pair[0][:, :],
                                   func=AF.Square),
                waits=[("t", mT), ("v", mred), ("v", st["m_ln2sq_v"])],
                drain=True)
            ms2 = K.s("activation", dict(out=sq_sb[:, 512:1024],
                                         in_=pair[1][:, :], func=AF.Square),
                      inc="s", drain=True)
            mrs = K.v("reduce_sum", dict(out=stats[:, 4:5], in_=sq_sb[:, :],
                                         axis=AXX),
                      waits=[("s", ms2)], inc="v", drain=True)
            st["m_ln2sq_v"] = mrs
            K.v("scalar_tensor_tensor",
                dict(out=stats[:, 6:7], in0=stats[:, 2:3],
                     scalar=1.0 / (D * D), in1=stats[:, 2:3],
                     op0=ALU.mult, op1=ALU.mult), drain=True)
            mvar = K.v("scalar_tensor_tensor",
                       dict(out=stats[:, 7:8], in0=stats[:, 4:5],
                            scalar=1.0 / D, in1=stats[:, 6:7],
                            op0=ALU.mult, op1=ALU.subtract),
                       inc="v", drain=True)
            K.s("activation", dict(out=stats[:, 8:9], in_=stats[:, 7:8],
                                   func=AF.Ln, bias=1e-5),
                waits=[("v", mvar)], drain=True)
            mrstd = K.s("activation", dict(out=stats[:, 9:10],
                                           in_=stats[:, 8:9],
                                           func=AF.Exp, scale=-0.5),
                        inc="s", drain=True)
            mbias = K.v("scalar_tensor_tensor",
                        dict(out=statsb[:, 1:2], in0=stats[:, 2:3],
                             scalar=1.0 / D, in1=stats[:, 9:10],
                             op0=ALU.mult, op1=ALU.mult),
                        waits=[("s", mrstd)], inc="v", drain=True)
            hsem = ("yo", "yp")
            for half in range(2):
                K.s("activation",
                    dict(out=out_sb[:, half * 512:(half + 1) * 512],
                         in_=pair[half][:, :], func=AF.Identity,
                         bias=statsb[:, 1:2], scale=stats[:, 9:10]),
                    waits=[("v", mbias),
                           (hsem[half], st["m_out_half"][half])],
                    inc=("s" if half == 1 else None), drain=True)
            mnorm = K.cnt("s")
            st["m_ln2_s"][pi] = mnorm
            if pi == 0:
                st["m_pb6_s"] = mnorm
                st["m_pb7_s"] = mnorm
            for half in range(2):
                st["m_out_half"][half] = K.y(
                    "dma_start",
                    dict(out=out[s, c * 128:(c + 1) * 128,
                                 half * 512:(half + 1) * 512],
                         in_=out_sb[:, half * 512:(half + 1) * 512]),
                    waits=([("s", mnorm)] if half == 0 else []),
                    inc=(hsem[half], 16))

        def emit_cwt(e2, cb2):
            mts4 = []
            for j in range(4):
                c4 = cb2 * 4 + j
                mts4.append(K.v(
                    "tensor_single_scalar",
                    dict(out=cwt_sb[:, j, :], in_=onespk[:, :],
                         scalar=cw_all[:, c4, e2 - 1:e2],
                         op=ALU.mult),
                    waits=([("t", st["m_cwt_mm"])] if j == 0 else []),
                    inc="v"))
            return mts4

        pend_ln2 = []
        m_acc_v = {}
        m_act_read = 0
        m_sg = [0, 0]
        m_actf = [0, 0]
        m_act_ft = [0] * NFT
        m_uu = [0, 0]
        m_cwb = [0, 0]
        mcb = 0
        for e in range(NUNIT):
            sgu = gu_sl[(2 * e) % 3]
            suu = gu_sl[(2 * e + 1) % 3]
            sdt = dt_sl[e % 2]
            for cb in range(NCB):
                if e > 0:
                    slot = cb % 2
                    mts4 = st["mts4"]
                    if mts4 is None:
                        mts4 = emit_cwt(e, cb)
                    st["mts4"] = None
                    for j in range(4):
                        K.t("matmul",
                            dict(out=pb[6][:, j * 128:(j + 1) * 128],
                                 lhsT=cwt_sb[:, j, :],
                                 rhs=identb[:, :], start=True, stop=True),
                            waits=[("v", mts4[j])]
                            + ([("s", st["m_pb6_s"])] if j == 0 else []),
                            inc=("t" if j == 3 else None))
                    st["m_cwt_mm"] = K.cnt("t")
                    mcb = K.s("copy", dict(out=cwb_sb[:, slot, :],
                                           in_=pb[6][:, :]),
                              waits=[("t", st["m_cwt_mm"])], inc="s")
                    st["m_pb6_s"] = mcb
                for ft in range(NFT):
                    wf = FT_W[ft]
                    bg = pb[0 + (ft % 2) * 2]
                    bu = pb[1 + (ft % 2) * 2]
                    for kc in range(8):
                        w = []
                        if kc == 0:
                            w = [st["ffnw_g"][e], ("s", m_sg[ft % 2])]
                        K.t("matmul",
                            dict(out=bg[0:wf, :],
                                 lhsT=sgu[:, kc, ft * 128:ft * 128 + wf],
                                 rhs=x1_fm[:, kc, cb * CB:(cb + 1) * CB],
                                 start=(kc == 0), stop=(kc == 7)),
                            waits=w, inc=("t" if kc == 7 else None))
                    mg = K.cnt("t")
                    for kc in range(8):
                        w = ([("v", m_uu[ft % 2]), st["ffnw_mark"][e]]
                             if kc == 0 else [])
                        K.t("matmul",
                            dict(out=bu[0:wf, :],
                                 lhsT=suu[:, kc, ft * 128:ft * 128 + wf],
                                 rhs=x1_fm[:, kc, cb * CB:(cb + 1) * CB],
                                 start=(kc == 0), stop=(kc == 7)),
                            waits=w, inc=("t" if kc == 7 else None))
                    mu = K.cnt("t")
                    msg = K.s("activation",
                              dict(out=sg_sb[0:wf, ft % 2, :], in_=bg[0:wf, :],
                                   func=AF.Silu),
                              waits=[("t", mg), ("v", m_actf[ft % 2])], inc="s")
                    m_sg[ft % 2] = msg
                    if e > 0:
                        muu = K.v("tensor_tensor",
                                  dict(out=uu_sb[0:wf, ft % 2, :],
                                       in0=bu[0:wf, :],
                                       in1=cwb_sb[0:wf, cb % 2, :],
                                       op=ALU.mult),
                                  waits=[("t", mu), ("s", mcb)], inc="v",
                                  drain=True)
                    else:
                        muu = K.v("tensor_copy",
                                  dict(out=uu_sb[0:wf, ft % 2, :],
                                       in_=bu[0:wf, :]),
                                  waits=[("t", mu)], inc="v", drain=True)
                    m_uu[ft % 2] = muu
                    if cb == NCB - 1 and ft == NFT - 1:
                        st["arena_free"][(2 * e) % 3] = K.cnt("t")
                        st["arena_free"][(2 * e + 1) % 3] = K.cnt("t")
                    m_actf[ft % 2] = K.v("tensor_tensor",
                        dict(out=act_sb[0:wf, ft, :],
                             in0=sg_sb[0:wf, ft % 2, :],
                             in1=uu_sb[0:wf, ft % 2, :], op=ALU.mult),
                        waits=[("s", msg), ("t", m_act_read)],
                        inc="v", drain=True)
                    m_act_ft[ft] = m_actf[ft % 2]
                    if ft % 2 == 0 and pend_ln2:
                        c_, wx_ = pend_ln2.pop(0)
                        ln2_chunk(c_, wx_, pi=0)
                m_act = K.cnt("v")
                if e > 0:
                    m_cwb[cb % 2] = m_act
                e2, cb2 = (e, cb + 1) if cb + 1 < NCB else (e + 1, 0)
                if e2 < NUNIT and e2 > 0:
                    st["mts4"] = emit_cwt(e2, cb2)
                for dm in range(8):
                    bank = pb[4 + dm % 2]
                    for ic in range(8):
                        wic = FT_W[ic]
                        w = [("v", m_act_ft[ic])]
                        if ic == 0:
                            w.append(("v", m_acc_v.get(dm % 2, 0)))
                            if e == NUNIT - 1 and dm < 2:
                                w += [("s", st["m_ln2_s"][1]),
                                      ("v", st["m_ln2_v"][1])]
                        K.t("matmul",
                            dict(out=bank[:, :],
                                 lhsT=sdt[0:wic, ic, dm * 128:(dm + 1) * 128],
                                 rhs=act_sb[0:wic, ic, :],
                                 start=(ic == 0), stop=(ic == 7)),
                            waits=w, inc=("t" if ic == 7 else None))
                    mdn = K.cnt("t")
                    in0 = (x1_fm[:, dm, cb * CB:(cb + 1) * CB] if e == 0
                           else ACC[:, dm, cb * CB:(cb + 1) * CB])
                    mad = K.v("tensor_tensor",
                              dict(out=ACC[:, dm, cb * CB:(cb + 1) * CB],
                                   in0=in0, in1=bank[:, :], op=ALU.add),
                              waits=[("t", mdn)],
                              inc="v", drain=True)
                    m_acc_v[dm % 2] = mad
                m_act_read = K.cnt("t")
                if cb == NCB - 1:
                    st["arena_free"][3 + e % 2] = K.cnt("t")
                if e == NUNIT - 1:
                    for j in range(4):
                        pend_ln2.append(
                            (cb * 4 + j,
                             ([("v", m_acc_v[0]), ("v", m_acc_v[1])]
                              if j == 0 else [])))
                    if cb == NCB - 1:
                        for j in range(len(pend_ln2)):
                            c_, wx_ = pend_ln2[j]
                            ln2_chunk(c_, wx_, pi=c_ % 2)
                        pend_ln2.clear()
            if e + 1 < NUNIT:
                load_unit_weights(e + 1, gen[0])
                gen[0] += 1
        m_units_t = K.cnt("t")
        m_units_v = K.cnt("v")
        st["m_x1fm_free"] = m_units_t
        st["m_cwfm_free"] = m_units_t

        if s == 0 and STAGE == "full":
            load_attn_weights(first=False)

    gen = [0]
    build_bias_tiles()
    load_attn_weights(first=True)
    phase_A(0)
    if STAGE != "a1":
        load_unit_weights(0, gen[0]); gen[0] += 1
        ffn(0)
        if STAGE == "full":
            phase_A(1)
            load_unit_weights(0, gen[0]); gen[0] += 1
            ffn(1)
    K.y("nop", dict(), waits=[("yo", K.cnt("yo")), ("yp", K.cnt("yp"))])
    K.check()

    sems_names = ["t", "v", "s", "gm", "gc", "ga", "gf0", "gf1",
                  "gf2", "yc", "yx0", "yx1", "yo", "gx0", "gx1", "gx2", "gx3", "gx4",
                  "gh0", "gh1", "gh2", "yp"]
    stack = contextlib.ExitStack()
    sems = {n: stack.enter_context(nc.semaphore(f"sem_{n}")) for n in sems_names}
    _replay_all(nc, K, sems)
    stack.close()
    return nc



def _prep_weights(inputs):
    f32 = np.float32
    GU = np.asarray(inputs["L0_ffn_shared_gate_up"], f32)
    EGU = np.asarray(inputs["L0_ffn_experts_gate_up"], f32)
    DN = np.asarray(inputs["L0_ffn_shared_down"], f32)
    EDN = np.asarray(inputs["L0_ffn_experts_down"], f32)
    gug = np.empty((NUNIT, D, I_FF), f32)
    guu = np.empty((NUNIT, D, I_FF), f32)
    dnt = np.empty((NUNIT, I_FF, D), f32)
    gug[0] = GU[:I_FF].T
    guu[0] = GU[I_FF:].T
    dnt[0] = DN.T
    for e in range(8):
        gug[e + 1] = EGU[e, :I_FF].T
        guu[e + 1] = EGU[e, I_FF:].T
        dnt[e + 1] = EDN[e].T
    def pt(w):
        return np.ascontiguousarray(
            np.asarray(w, f32).reshape(8, 128, D).transpose(1, 0, 2))

    return {
        "wq": pt(inputs["L0_attn_Wq"]),
        "wk": pt(inputs["L0_attn_Wk"]),
        "wv": pt(inputs["L0_attn_Wv"]),
        "wo": pt(inputs["L0_attn_Wo"]),
        "bq": np.ascontiguousarray(np.asarray(inputs["L0_attn_bq"], f32)),
        "bk": np.ascontiguousarray(np.asarray(inputs["L0_attn_bk"], f32)),
        "bv": np.ascontiguousarray(np.asarray(inputs["L0_attn_bv"], f32)),
        "bo": np.ascontiguousarray(np.asarray(inputs["L0_attn_bo"], f32)),
        "gwt": np.ascontiguousarray(np.asarray(inputs["L0_ffn_gate_weight"],
                                               f32).T),
        "gug": np.ascontiguousarray(
            gug.reshape(NUNIT, 8, 128, I_FF).transpose(0, 2, 1, 3)),
        "guu": np.ascontiguousarray(
            guu.reshape(NUNIT, 8, 128, I_FF).transpose(0, 2, 1, 3)),
        "dnt": np.ascontiguousarray(
            dnt[:, :896].reshape(NUNIT, 7, 128, D).transpose(0, 2, 1, 3)),
        "dntt": np.ascontiguousarray(dnt[:, 896:938]),
        "cmask": np.ascontiguousarray(
            (np.arange(128)[:, None] <= np.arange(128)[None, :]).astype(f32)),
        "identf": np.eye(128, dtype=f32),
        "identb": np.eye(128, dtype=f32).astype(ml_dtypes.bfloat16),
        "onespk": np.ones((128, 128), ml_dtypes.bfloat16),
    }


def kernel(**inputs):
    x = np.asarray(inputs["L0_attn_x"], np.float32)
    common = _prep_weights(inputs)
    nc = build_nc()
    in_maps = []
    for c in range(NCORE):
        m = dict(common)
        m["xin"] = np.ascontiguousarray(x[2 * c:2 * c + 2])
        in_maps.append(m)
    res = run_bass_kernel_spmd(nc, in_maps, core_ids=list(range(NCORE)))
    outs = [np.asarray(res.results[c]["out"]).reshape(2, SEQ, D)
            for c in range(NCORE)]
    return np.concatenate(outs, axis=0)


# version 36
# speedup vs baseline: 1.0541x; 1.0541x over previous
import contextlib
import os
import numpy as np
import ml_dtypes
import concourse.bass as bass
import concourse.mybir as mybir
from concourse.bass_utils import run_bass_kernel_spmd

F32 = mybir.dt.float32
BF16 = mybir.dt.bfloat16
AF = mybir.ActivationFunctionType
ALU = mybir.AluOpType
AXX = mybir.AxisListType.X

NCORE = 8
SEQ = 2048
D = 1024
H = 16
NCHUNK = 16
GRP = 512
NGRP = 4
CPG = 4
I_FF = 938
NFT = 8
FT_W = [128] * 7 + [42]
NUNIT = 9
CB = 512
NCB = 4

STAGE = "full"


class KB:
    ENGS = ("t", "v", "s", "g", "y")
    DMA_SEMS = ("gc", "ga", "gf0", "gf1", "gf2",
                "yc", "yx0", "yx1", "yo",
                "gx0", "gx1", "gx2", "gx3", "gx4", "gh0", "gh1", "gh2", "yp", "gb")

    def __init__(self):
        self.q = {e: [] for e in self.ENGS}
        self.led = {}
        self.waited = {}

    def cnt(self, sem):
        return self.led.get(sem, 0)

    SIM = (os.environ.get("BASS_SIM_FP64") == "1"
           or os.environ.get("KBDRAIN") == "1")
    VDRAIN_HW = True
    SDRAIN_HW = False

    def _emit(self, eng, meth, kwargs, waits=(), inc=None, drain=False):
        if drain and not self.SIM:
            drain = self.VDRAIN_HW if eng == "v" else self.SDRAIN_HW
        wl = []
        for sem, val in waits:
            if val is None or val <= 0:
                continue
            if sem in self.DMA_SEMS:
                assert val == self.led.get(sem, 0), (
                    f"dma wait not at running max: {sem} {val} "
                    f"!= {self.led.get(sem, 0)}")
            key = (eng, sem)
            if self.waited.get(key, 0) >= val:
                continue
            self.waited[key] = val
            wl.append((sem, val))
        mark = None
        incs = []
        if inc is not None:
            sem, k = inc if isinstance(inc, tuple) else (inc, 1)
            if sem in self.DMA_SEMS:
                self.led[sem] = self.led.get(sem, 0) + 1
                mark = self.led[sem]
                incs.append((sem, 16))
            else:
                self.led[sem] = self.led.get(sem, 0) + k
                mark = self.led[sem]
                incs.append((sem, k))
        self.q[eng].append((wl, drain, meth, kwargs, incs))
        return mark

    def t(self, meth, kwargs, waits=(), inc=None):
        return self._emit("t", meth, kwargs, waits, inc)

    def v(self, meth, kwargs, waits=(), inc=None, drain=False):
        return self._emit("v", meth, kwargs, waits, inc, drain)

    def s(self, meth, kwargs, waits=(), inc=None, drain=False):
        return self._emit("s", meth, kwargs, waits, inc, drain)

    def g(self, meth, kwargs, waits=(), inc=None):
        return self._emit("g", meth, kwargs, waits, inc)

    def y(self, meth, kwargs, waits=(), inc=None):
        return self._emit("y", meth, kwargs, waits, inc)

    def dummy(self, waits=(), tile="pw"):
        bt = self.barrier_tiles
        self.t("matmul", dict(out=bt[tile], lhsT=bt["ib"], rhs=bt["ib"],
                              start=True, stop=True), waits=waits)

    def barrier(self):
        bt = self.barrier_tiles
        gm0 = self.cnt("gm")
        mv = self.v("tensor_copy", dict(out=bt["v"], in_=bt["z"]),
                    waits=[("gm", gm0)], inc="v", drain=True)
        ms = self.s("copy", dict(out=bt["s"], in_=bt["z"]),
                    waits=[("gm", gm0)], inc="s", drain=True)
        mg = self.g("memset", dict(ap=bt["g"], constant=0.0), inc=("gm", 1))
        mt = self.t("matmul", dict(out=bt["pt"], lhsT=bt["ib"], rhs=bt["ib"],
                                   start=True, stop=True),
                    waits=[("v", mv), ("s", ms)], inc="t")
        comp = [("t", mt), ("v", mv), ("s", ms), ("gm", mg)]
        dma = [(s_, self.cnt(s_)) for s_ in self.DMA_SEMS]
        for e in ("g", "y"):
            self._emit(e, "nop", dict(), waits=comp)
        for e in ("t", "v", "s"):
            self._emit(e, "nop", dict(), waits=comp + dma)

    def check(self):
        for (eng, sem), val in self.waited.items():
            assert val <= self.led.get(sem, 0), (
                f"hang: {eng} waits {sem}>={val} ledger={self.led.get(sem, 0)}")
            if sem in self.DMA_SEMS:
                assert eng not in ("g", "y") or sem in ("yo", "yp"), (eng, sem)


def _replay_all(nc, K, sems):
    resolved = {}
    actual = {}
    legs_box = [0]

    def cb(ins):
        if type(ins).__name__ == "InstDMACopy":
            legs_box[0] += 1

    nc.push_inst_callback(cb)

    def run(engine, items, resolve_dma):
        for waits, drain, meth, kwargs, incs in items:
            for sem, val in waits:
                if sem in KB.DMA_SEMS:
                    if not resolve_dma:
                        raise AssertionError(f"unresolvable dma wait {sem}")
                    engine.wait_ge(sems[sem], resolved[(sem, val)])
                else:
                    engine.wait_ge(sems[sem], val)
            if drain:
                engine.drain()
            legs_box[0] = 0
            ins = getattr(engine, meth)(**kwargs)
            for sem, k in incs:
                ins.then_inc(sems[sem], k)
                if sem in KB.DMA_SEMS:
                    legs = max(legs_box[0], 1)
                    actual[sem] = actual.get(sem, 0) + 16 * legs
                    resolved[(sem, K_ids[sem])] = actual[sem]
                    K_ids[sem] += 1

    K_ids = {s: 1 for s in KB.DMA_SEMS}

    with nc.Block() as block:
        @block.gpsimd
        def _(G):
            run(G, K.q["g"], resolve_dma=False)

        @block.sync
        def _(Y):
            run(Y, K.q["y"], resolve_dma=True)

        @block.tensor
        def _(T):
            run(T, K.q["t"], resolve_dma=True)

        @block.vector
        def _(V):
            run(V, K.q["v"], resolve_dma=True)

        @block.scalar
        def _(S):
            run(S, K.q["s"], resolve_dma=True)


def _tp(row, col):
    return (row, col) if (row or col) else None


def build_nc():
    nc = bass.Bass("TRN2", target_bir_lowering=False, debug=False)

    xin = nc.dram_tensor("xin", [2, SEQ, D], F32, kind="ExternalInput").ap()
    out = nc.dram_tensor("out", [2, SEQ, D], F32, kind="ExternalOutput").ap()
    wq_h = nc.dram_tensor("wq", [128, 8, D], F32, kind="ExternalInput").ap()
    wk_h = nc.dram_tensor("wk", [128, 8, D], F32, kind="ExternalInput").ap()
    wv_h = nc.dram_tensor("wv", [128, 8, D], F32, kind="ExternalInput").ap()
    wo_h = nc.dram_tensor("wo", [128, 8, D], F32, kind="ExternalInput").ap()
    bq_h = nc.dram_tensor("bq", [D], F32, kind="ExternalInput").ap()
    bk_h = nc.dram_tensor("bk", [D], F32, kind="ExternalInput").ap()
    bv_h = nc.dram_tensor("bv", [D], F32, kind="ExternalInput").ap()
    bo_h = nc.dram_tensor("bo", [D], F32, kind="ExternalInput").ap()
    gwt_h = nc.dram_tensor("gwt", [D, 8], F32, kind="ExternalInput").ap()
    gug_h = nc.dram_tensor("gug", [NUNIT, 128, 8, I_FF], F32,
                           kind="ExternalInput").ap()
    guu_h = nc.dram_tensor("guu", [NUNIT, 128, 8, I_FF], F32,
                           kind="ExternalInput").ap()
    dnt_h = nc.dram_tensor("dnt", [NUNIT, 128, 7, D], F32,
                           kind="ExternalInput").ap()
    dnt_t = nc.dram_tensor("dntt", [NUNIT, 42, D], F32,
                           kind="ExternalInput").ap()
    cmask_h = nc.dram_tensor("cmask", [128, 128], F32, kind="ExternalInput").ap()
    identf_h = nc.dram_tensor("identf", [128, 128], F32, kind="ExternalInput").ap()
    identb_h = nc.dram_tensor("identb", [128, 128], BF16, kind="ExternalInput").ap()
    onespk_h = nc.dram_tensor("onespk", [128, 128], BF16, kind="ExternalInput").ap()

    TOTAL = 212700
    _before = nc._sbuf_addr_for_side("left")
    nc.alloc_sbuf_tensor("reserved_arena", [128, TOTAL], mybir.dt.int8)
    _after = nc._sbuf_addr_for_side("left")
    base = _after - TOTAL
    cur = [base]

    def alloc(name, shape, dtype, at=None):
        nbytes = int(np.prod(shape[1:])) * mybir.dt.size(dtype)
        if at is None:
            off = (cur[0] + 31) & ~31
            cur[0] = off + nbytes
        else:
            off = at
        return nc.alloc_sbuf_tensor_at(name, list(shape), dtype, offset=off).ap()

    x1_fm = alloc("x1_fm", [128, 8, SEQ], BF16)

    arena_base = (cur[0] + 31) & ~31
    GUSL = 8 * I_FF * 2
    DTSL = 8 * D * 2
    gu_off = [arena_base, arena_base + GUSL, arena_base + 2 * GUSL + DTSL]
    gu_sl = [alloc(f"gusl{i}", [128, 8, I_FF], BF16, at=gu_off[i])
             for i in range(3)]
    dt_off = [arena_base + 2 * GUSL, arena_base + 3 * GUSL + DTSL]
    dt_sl = [alloc(f"dtsl{i}", [128, 8, D], BF16, at=dt_off[i])
             for i in range(2)]
    arena_end = arena_base + 3 * GUSL + 2 * DTSL
    wq_sb = alloc("wq_sb", [128, 8, D], BF16, at=arena_base + 0 * 16384)
    wk_sb = alloc("wk_sb", [128, 8, D], BF16, at=arena_base + 1 * 16384)
    wv_sb = alloc("wv_sb", [128, 8, D], BF16, at=arena_base + 2 * 16384)
    wo_sb = alloc("wo_sb", [128, 8, D], BF16, at=arena_base + 3 * 16384)
    cur[0] = arena_end

    r1_base = (cur[0] + 31) & ~31
    ACC = alloc("ACC", [128, 8, SEQ], F32, at=r1_base)
    pa = [r1_base]

    def palloc(name, shape, dtype):
        nbytes = int(np.prod(shape[1:])) * mybir.dt.size(dtype)
        off = (pa[0] + 31) & ~31
        pa[0] = off + nbytes
        return nc.alloc_sbuf_tensor_at(name, list(shape), dtype, offset=off).ap()

    x_fm = palloc("x_fm", [128, 8, GRP], BF16)
    q_fm = palloc("q_fm", [128, 8, GRP], BF16)
    k_fm = palloc("k_fm", [128, 8, GRP], BF16)
    k_tm = palloc("k_tm", [128, 2, D], BF16)
    v_tm = palloc("v_tm", [128, CPG, H * 65], BF16)
    x_res = palloc("x_res", [128, 5, D], BF16)
    r1_sb = palloc("r1_sb", [128, D], F32)
    x1_tmb = palloc("x1_tmb", [128, D], BF16)
    A_mT = palloc("A_mT", [128, 16, 128], BF16)
    val_tm = palloc("val_tm", [128, D], BF16)
    val_fm = palloc("val_fm", [128, 8, 128], BF16)
    S_f32 = palloc("S_f32", [128, 8, 65], F32)
    S_bf = palloc("S_bf", [128, 8, 65], BF16)
    den_t = palloc("den_t", [128, 16], F32)
    r_den = palloc("r_den", [128, 16], F32)
    assert pa[0] - r1_base <= 8 * SEQ * 4, f"R1 overflow {pa[0] - r1_base}"
    cur[0] = r1_base + 8 * SEQ * 4

    act_sb = alloc("act_sb", [128, 8, CB], BF16)
    cwb_sb = alloc("cwb_sb", [128, 2, CB], BF16)
    cw_all = alloc("cw_all", [128, 16, 8], F32)
    cwt_sb = alloc("cwt_sb", [128, 4, 128], BF16)
    onespk = alloc("onespk", [128, 128], BF16)
    sq_sb = alloc("sq_sb", [128, D], F32)
    out_sb = alloc("out_sb", [128, D], F32)
    p_sb = alloc("p_sb", [128, GRP], BF16)
    n_sb = alloc("n_sb", [128, GRP], BF16)
    en_sb = alloc("en_sb", [128, GRP], BF16)
    sg_sb = alloc("sg_sb", [128, 2, CB], BF16)
    uu_sb = alloc("uu_sb", [128, 2, CB], BF16)
    stats = alloc("stats", [128, 12], F32)
    statsb = alloc("statsb", [128, 4], F32)
    g4 = alloc("g4", [128, 152], F32)
    e_sb = alloc("e_sb", [128, 8], F32)
    eq1_sb = alloc("eq1_sb", [128, 8], F32)
    eq2_sb = alloc("eq2_sb", [128, 8], F32)
    em_sb = alloc("em_sb", [128, 8], F32)
    cw_sb = alloc("cw_sb", [128, 8], F32)
    gt_sb = alloc("gt_sb", [128, 8], F32)
    bq_sb = alloc("bq_sb", [128, 8], F32)
    bk_sb = alloc("bk_sb", [128, 8], F32)
    bv_sb = alloc("bv_sb", [128, 8], F32)
    bo_sb = alloc("bo_sb", [128, 8], F32)
    bvb = alloc("bvb", [128, D], BF16)
    bkb = alloc("bkb", [128, D], BF16)
    bob = alloc("bob", [128, D], BF16)
    bbt = alloc("bbt", [128, 128], BF16)
    gwt_sb = alloc("gwt_sb", [128, 8, 8], BF16)
    cmask = alloc("cmask", [128, 128], F32)
    identf = alloc("identf", [128, 128], F32)
    identb = alloc("identb", [128, 128], BF16)
    cvals = alloc("cvals", [128, 8], F32)
    nc.const_aps.register(F32, 1e-5, cvals[:, 0:1])
    nc.const_aps.register(F32, -0.5, cvals[:, 1:2])
    nc.const_aps.register(F32, 0.0, cvals[:, 2:3])
    nc.const_aps.register(F32, 1e-6, cvals[:, 6:7])
    assert cur[0] - base <= TOTAL, f"SBUF overflow {cur[0] - base}"

    pb = [nc.alloc_psum_tensor(f"pb{i}", [128, 512], F32) for i in range(8)]

    K = KB()
    K.barrier_tiles = {"v": cvals[:, 3:4], "s": cvals[:, 4:5],
                       "g": cvals[:, 5:6], "z": cvals[:, 2:3],
                       "ib": identb[:, 0:16],
                       "pt": pb[0][0:16, 424:440],
                       "pw": pb[7][0:16, 496:512]}

    for dst, src in ((cmask, cmask_h), (identf, identf_h), (identb, identb_h),
                     (onespk, onespk_h)):
        K.y("dma_start", dict(out=dst[:, :], in_=src[:, :]), inc=("yc", 16))
    K.y("dma_start", dict(out=bq_sb[:, :],
                          in_=bq_h.rearrange("(g p) -> p g", p=128),
                          allow_slow_non_contiguous=True), inc=("yc", 16))
    K.y("dma_start", dict(out=bk_sb[:, :],
                          in_=bk_h.rearrange("(g p) -> p g", p=128),
                          allow_slow_non_contiguous=True), inc=("yc", 16))
    K.y("dma_start", dict(out=bv_sb[:, :],
                          in_=bv_h.rearrange("(g p) -> p g", p=128),
                          allow_slow_non_contiguous=True), inc=("yc", 16))
    K.y("dma_start", dict(out=bo_sb[:, :],
                          in_=bo_h.rearrange("(g p) -> p g", p=128),
                          allow_slow_non_contiguous=True), inc=("yc", 16))
    K.g("dma_start", dict(out=gwt_sb[:, :, :],
                          in_=gwt_h.rearrange("(kc p) e -> p kc e", p=128)),
        inc=("gc", 16))
    K.g("memset", dict(ap=cvals[:, 0:1], constant=1e-5), inc=("gm", 1))
    K.g("memset", dict(ap=cvals[:, 1:2], constant=-0.5), inc=("gm", 1))
    K.g("memset", dict(ap=cvals[:, 2:3], constant=0.0), inc=("gm", 1))
    K.g("memset", dict(ap=cvals[:, 6:7], constant=1e-6), inc=("gm", 1))

    m_bias_chain = {}
    bidx = [0]

    def build_bias_tiles():
        for b_sb_, dstb in ((bk_sb, bkb), (bv_sb, bvb), (bo_sb, bob)):
            for gcol in range(8):
                i = bidx[0]
                bidx[0] += 1
                mts = K.v("tensor_single_scalar",
                          dict(out=bbt[:, :], in_=onespk[:, :],
                               scalar=b_sb_[:, gcol:gcol + 1], op=ALU.mult),
                          waits=[("yc", K.cnt("yc")),
                                 ("t", m_bias_chain.get(i - 1, 0))],
                          inc="v", drain=True)
                mmm = K.t("matmul",
                          dict(out=pb[6][:, 0:128], lhsT=bbt[:, :],
                               rhs=identb[:, :], start=True, stop=True),
                          waits=[("v", mts),
                                 ("s", m_bias_chain.get((i - 1, "s"), 0))],
                          inc="t")
                m_bias_chain[i] = mmm
                mcp = K.s("copy",
                          dict(out=dstb[:, gcol * 128:(gcol + 1) * 128],
                               in_=pb[6][:, 0:128]),
                          waits=[("t", mmm)], inc="s")
                m_bias_chain[(i, "s")] = mcp
        st["m_biasb"] = K.cnt("s")
        st["m_pb6_s"] = K.cnt("s")

    st = {
        "arena_free": [0, 0, 0, 0, 0],
        "m_biasb": 0,
        "m_gate_v": 0,
        "m_Aevac": [0, 0], "m_sqread": 0,
        "x1e_par": [0, 0], "m_projv": 0, "m_x1t_mm": 0,
        "m_xfm_last": 0, "m_valfm_last": 0,
        "attnw_mark": 0,
        "ffnw_mark": {}, "ffnw_g": {},
        "mr1": 0, "mV2": 0, "mscopy": 0, "mstate": 0,
        "mx1evac": 0, "mWo": 0, "mnorm": 0,
        "mvA": {}, "m_dagrp": {},
        "xres_free": [(None, 0)] * 5,
        "mqk_last_read": 0,
        "m_x1fm_free": 0, "m_cwfm_free": 0,
        "m_LN2bank": {}, "m_ln2sq_v": 0,
        "m_ln2_s": [0, 0], "m_ln2_v": [0, 0], "m_out_half": [0, 0],
        "m_out_store": 0,
        "m_pb6_v": 0, "m_pb7_v": 0, "m_pb6_s": 0, "m_pb7_s": 0,
        "m_exp": 0, "m_tt": 0,
        "mV3_last": 0, "m_ktm_last": 0, "m_Aevac_last": [0, 0],
        "m_xb": [0] * 5, "m_cwt_mm": 0, "mr1h": 0, "m_ln1red": 0,
        "mts4": None,
        "x1e4": [[0] * 4, [0] * 4],
    }

    def load_attn_weights(first):
        af = st["arena_free"]
        order = ((wq_h, wq_sb, (0, 1), "ga"), (wo_h, wo_sb, (2, 4), "ga"),
                 (wk_h, wk_sb, (1, 3), "gb"), (wv_h, wv_sb, (3, 2), "gb"))
        for hbm, sb, idx, sem in order:
            waits = [] if first else [("t", max(af[i] for i in idx))]
            K.g("dma_start",
                dict(out=sb[:, :, :], in_=hbm[:, :, :]),
                waits=waits, inc=(sem, 16))
        st["attnw_mark"] = K.cnt("ga")
        st["attnw_kv"] = K.cnt("gb")

    def load_unit_weights(e, gen_i):
        sm = f"gf{gen_i % 3}"
        K.g("dma_start",
            dict(out=gu_sl[(2 * e) % 3][:, :, :], in_=gug_h[e]),
            waits=[("t", st["arena_free"][(2 * e) % 3])], inc=(sm, 16))
        K.g("dma_start",
            dict(out=dt_sl[e % 2][:, 0:7, :], in_=dnt_h[e]),
            waits=[("t", st["arena_free"][3 + e % 2])], inc=(sm, 16))
        K.g("dma_start", dict(out=dt_sl[e % 2][0:42, 7, :],
                              in_=dnt_t[e]), inc=(sm, 16))
        st["ffnw_g"][e] = (sm, K.cnt(sm))
        sm2 = f"gh{gen_i % 3}"
        K.g("dma_start",
            dict(out=gu_sl[(2 * e + 1) % 3][:, :, :], in_=guu_h[e]),
            waits=[("t", st["arena_free"][(2 * e + 1) % 3])], inc=(sm2, 16))
        st["ffnw_mark"][e] = (sm2, K.cnt(sm2))

    def elu1_evac(bank, bias_col, dst, n):
        if bias_col is not None:
            K.v("tensor_scalar", dict(out=p_sb[:, 0:n], in0=bank,
                                      scalar1=bias_col, scalar2=0.0,
                                      op0=ALU.add, op1=ALU.max), drain=True)
            mn = K.v("tensor_scalar", dict(out=n_sb[:, 0:n], in0=bank,
                                           scalar1=bias_col, scalar2=0.0,
                                           op0=ALU.add, op1=ALU.min),
                     waits=[("s", st["m_exp"])], inc="v")
        else:
            K.v("tensor_single_scalar", dict(out=p_sb[:, 0:n], in_=bank,
                                             scalar=0.0, op=ALU.max), drain=True)
            mn = K.v("tensor_single_scalar", dict(out=n_sb[:, 0:n], in_=bank,
                                                  scalar=0.0, op=ALU.min),
                     waits=[("s", st["m_exp"])], inc="v")
        me = K.s("activation", dict(out=en_sb[:, 0:n], in_=n_sb[:, 0:n],
                                    func=AF.Exp),
                 waits=[("v", mn), ("v", st["m_tt"])], inc="s")
        st["m_exp"] = me
        mq = K.v("tensor_tensor", dict(out=dst, in0=p_sb[:, 0:n],
                                       in1=en_sb[:, 0:n], op=ALU.add),
                 waits=[("s", me), ("t", st["mqk_last_read"])],
                 inc="v", drain=True)
        st["m_tt"] = mq
        return mn, mq

    def phase_A(s):
        K.barrier()
        v_ones = v_tm.rearrange("p c (h m) -> p c h m", m=65)[:, :, :, 64:65]
        K.g("memset", dict(ap=v_ones, constant=1.0), inc=("gm", 1))
        m_ones = K.cnt("gm")
        pend = {}
        apend = {}
        pend2 = {}

        def pbase(c):
            return 2 if c % 2 == 0 else 4

        def num_slice(h_, c):
            if h_ < 7:
                return pb[6][:, h_ * 65:h_ * 65 + 65]
            if h_ < 14:
                return pb[7][:, (h_ - 7) * 65:(h_ - 7) * 65 + 65]
            off = 440 if c % 2 == 0 else 375
            return pb[h_ - 14][:, off:off + 65]

        def stage1_A(c):
            cc = c % CPG
            B = pbase(c)
            p = {}
            mwave = {}
            for w8 in range(2):
                for hh in range(8):
                    h = 8 * w8 + hh
                    b = (h % 2) * 64
                    pair = h // 2
                    Ab = pb[B + h % 2][:, ((h // 2) % 4) * 128:
                                      ((h // 2) % 4) * 128 + 128]
                    if w8 == 0:
                        bk4 = (h % 2) * 2 + ((h // 2) % 4) // 2
                        w = [("v", st["m_tt"]), ("gm", m_ones),
                             ("s", st["x1e4"][c % 2][bk4])]
                    else:
                        w = [("v", st["m_tt"]), ("gm", m_ones),
                             ("v", mwave.get((w8 - 1, h % 2), 0))]
                    K.t("matmul",
                        dict(out=Ab,
                             lhsT=k_fm[b:b + 64, pair,
                                       cc * 128:(cc + 1) * 128],
                             rhs=q_fm[b:b + 64, pair,
                                      cc * 128:(cc + 1) * 128],
                             start=True, stop=True,
                             tile_position=_tp(b, 0)),
                        waits=w, inc=("t" if hh >= 6 else None))
                    if hh >= 6:
                        mwave[("t", w8, h % 2)] = K.cnt("t")
                for par in range(2):
                    mwave[(w8, par)] = K.v(
                        "tensor_tensor",
                        dict(out=A_mT[:, 8 * par + 4 * w8:
                                      8 * par + 4 * w8 + 4, :],
                             in0=pb[B + par][:, :].rearrange(
                                 "p (a m) -> p a m", m=128),
                             in1=cmask.unsqueeze(1).broadcast_to(
                                 [128, 4, 128]),
                             op=ALU.mult),
                        waits=[("t", mwave[("t", w8, par)])], inc="v")
            p["mwave"] = mwave
            st["m_Aevac_last"] = [mwave[(1, 0)], mwave[(1, 1)]]
            apend[c] = p

        def stage1_kT(c):
            cc = c % CPG
            B = pbase(c)
            p = apend[c]
            mwave = p["mwave"]
            mkt = {}
            for gg in range(8):
                bank = pb[B + gg // 4][:, (gg % 4) * 128:(gg % 4) * 128 + 128]
                w = [("v", mwave[(1, 0)]), ("v", mwave[(1, 1)])]
                mm = K.t("matmul",
                         dict(out=bank,
                              lhsT=k_fm[:, gg, cc * 128:(cc + 1) * 128],
                              rhs=identb[:, :], start=True, stop=True),
                         waits=w, inc=("t" if gg % 4 == 3 else None))
                if gg % 4 == 3:
                    mkt[("t", gg // 4)] = mm
            for bk2 in range(2):
                K.s("copy",
                    dict(out=k_tm[:, c % 2, bk2 * 512:(bk2 + 1) * 512],
                         in_=pb[B + bk2][:, 0:512]),
                    waits=[("t", mkt[("t", bk2)])], inc="s")
            m_ktm = K.cnt("s")
            p["m_ktm"] = m_ktm
            st["m_ktm_last"] = m_ktm

        def stage1_rest(c):
            cc = c % CPG
            B = pbase(c)
            p = apend.pop(c)
            mwave = p["mwave"]
            m_ktm = p["m_ktm"]
            for h in range(H):
                pair = h // 2
                w = [("s", m_ktm)]
                if h % 8 == 0:
                    w.append(("s", st["mscopy"]))
                K.t("matmul",
                    dict(out=pb[h // 8][(h % 2) * 64:(h % 2) * 64 + 64,
                                        (pair % 4) * 65:
                                        (pair % 4) * 65 + 65],
                         lhsT=k_tm[:, c % 2, h * 64:(h + 1) * 64],
                         rhs=v_tm[:, cc, h * 65:(h + 1) * 65],
                         start=True, stop=True,
                         tile_position=_tp(0, (h % 2) * 64)),
                    waits=w, inc=("t" if h == 15 else None))
            st["mstate"] = K.cnt("t")
            for h in range(H):
                b = (h % 2) * 64
                pair = h // 2
                w8 = h // 8
                aslot = 8 * (h % 2) + 4 * w8 + (h // 2) % 4
                nb = num_slice(h, c)
                if h < 7:
                    w = [("v", mwave[(w8, h % 2)]), ("s", st["mr1h"]),
                         ("s", st["m_pb6_s"])]
                elif h < 14:
                    w = [("v", mwave[(w8, h % 2)]), ("s", st["mr1"]),
                         ("s", st["m_pb7_s"])]
                else:
                    w = [("v", mwave[(w8, h % 2)]), ("s", st["mr1"]),
                         ("s", st["mscopy"])]
                K.t("matmul",
                    dict(out=nb, lhsT=A_mT[:, aslot, :],
                         rhs=v_tm[:, cc, h * 65:(h + 1) * 65],
                         start=True, stop=(c == 0)),
                    waits=w,
                    inc=("t" if (c == 0 and h == 15) else None))
                if c > 0:
                    K.t("matmul",
                        dict(out=nb,
                             lhsT=q_fm[b:b + 64, pair,
                                       cc * 128:(cc + 1) * 128],
                             rhs=S_bf[b:b + 64, pair, :],
                             start=False, stop=True,
                             tile_position=_tp(b, 0)),
                        waits=[("s", st["mscopy"])],
                        inc=("t" if h == 15 else None))
            m_nend = K.cnt("t")
            p["m_Aevac"] = [mwave[(1, 0)], mwave[(1, 1)]]
            m_num_all = m_nend
            offc = 440 if c % 2 == 0 else 375
            K.v("tensor_single_scalar",
                dict(out=den_t[:, 0:7],
                     in_=pb[6][:, 0:455].rearrange(
                         "p (h m) -> p h m", m=65)[:, :, 64],
                     scalar=1e-6, op=ALU.add),
                waits=[("t", m_num_all)], drain=True)
            K.v("tensor_single_scalar",
                dict(out=den_t[:, 7:14],
                     in_=pb[7][:, 0:455].rearrange(
                         "p (h m) -> p h m", m=65)[:, :, 64],
                     scalar=1e-6, op=ALU.add), drain=True)
            K.v("tensor_single_scalar",
                dict(out=den_t[:, 14:15], in_=pb[0][:, offc + 64:offc + 65],
                     scalar=1e-6, op=ALU.add), drain=True)
            K.v("tensor_single_scalar",
                dict(out=den_t[:, 15:16], in_=pb[1][:, offc + 64:offc + 65],
                     scalar=1e-6, op=ALU.add), drain=True)
            p["mV2"] = K.v("reciprocal", dict(out=r_den[:, :],
                                              in_=den_t[:, :]),
                           inc="v", drain=True)
            K.v("tensor_tensor",
                dict(out=val_tm[:, 0:448].rearrange("p (h m) -> p h m", m=64),
                     in0=pb[6][:, 0:455].rearrange(
                         "p (h m) -> p h m", m=65)[:, :, 0:64],
                     in1=r_den[:, 0:7].unsqueeze(-1).broadcast_to(
                         [128, 7, 64]),
                     op=ALU.mult), drain=True)
            K.v("tensor_tensor",
                dict(out=val_tm[:, 448:896].rearrange("p (h m) -> p h m", m=64),
                     in0=pb[7][:, 0:455].rearrange(
                         "p (h m) -> p h m", m=65)[:, :, 0:64],
                     in1=r_den[:, 7:14].unsqueeze(-1).broadcast_to(
                         [128, 7, 64]),
                     op=ALU.mult), drain=True)
            K.v("tensor_single_scalar",
                dict(out=val_tm[:, 896:960], in_=pb[0][:, offc:offc + 64],
                     scalar=r_den[:, 14:15], op=ALU.mult), drain=True)
            p["mV3"] = K.v("tensor_single_scalar",
                           dict(out=val_tm[:, 960:1024],
                                in_=pb[1][:, offc:offc + 64],
                                scalar=r_den[:, 15:16], op=ALU.mult),
                           inc="v", drain=True)
            madd = {}
            for bk2 in range(2):
                psrc = pb[bk2][:, 0:260]
                dstf = S_f32[:, 4 * bk2:4 * bk2 + 4, :]
                if bk2 == 0:
                    K.v("nop", dict(),
                        waits=[("t", st["mstate"]), ("s", st["mscopy"])])
                if c == 0:
                    madd[bk2] = K.v("tensor_copy",
                                    dict(out=dstf, in_=psrc.rearrange(
                                        "p (a m) -> p a m", m=65)),
                                    inc="v", drain=True)
                else:
                    madd[bk2] = K.v(
                        "tensor_tensor",
                        dict(out=dstf, in0=dstf,
                             in1=psrc.rearrange("p (a m) -> p a m", m=65),
                             op=ALU.add), inc="v", drain=True)
            p["madd"] = madd
            st["mV3_last"] = p["mV3"]
            pend[c] = p
            pend2[c] = p

        def stage1_scast(c):
            p = pend2.pop(c)
            madd = p["madd"]
            for bk2 in range(2):
                K.s("copy",
                    dict(out=S_bf[:, 4 * bk2:4 * bk2 + 4, :],
                         in_=S_f32[:, 4 * bk2:4 * bk2 + 4, :]),
                    waits=[("v", madd[bk2])],
                    inc=("s" if bk2 == 1 else None), drain=True)
            st["mscopy"] = K.cnt("s")

        def stage2a(c):
            cc = c % CPG
            B = pbase(c)
            p = pend[c]
            mvT = {}
            for tt in range(8):
                bank = pb[B + tt // 4][:, (tt % 4) * 128:(tt % 4) * 128 + 128]
                w = [("v", p["mV3"]), ("v", p["m_Aevac"][0]),
                     ("v", p["m_Aevac"][1]), ("s", p["m_ktm"]),
                     ("s", st["m_xfm_last"]), ("yc", K.cnt("yc"))]
                mm = K.t("matmul",
                         dict(out=bank,
                              lhsT=val_tm[:, tt * 128:(tt + 1) * 128],
                              rhs=identb[:, :], start=True, stop=True),
                         waits=w, inc=("t" if tt % 4 == 3 else None))
                if tt % 4 == 3:
                    mvT[("t", tt // 4)] = mm
            for bk2 in range(4):
                mvT[bk2] = K.s(
                    "copy",
                    dict(out=val_fm[:, 2 * bk2:2 * bk2 + 2, :],
                         in_=pb[B + bk2 // 2][:, (bk2 % 2) * 256:
                                              (bk2 % 2) * 256 + 256].rearrange(
                             "p (a m) -> p a m", m=128)),
                    waits=[("t", mvT[("t", bk2 // 2)])],
                    inc="s")
            m_valfm = K.cnt("s")
            p["m_valfm"] = m_valfm
            st["m_valfm_last"] = m_valfm

        def stage2b(c):
            cc = c % CPG
            p = pend[c]
            m_valfm = p["m_valfm"]
            for half in range(2):
                bank = pb[6 + half]
                for mc in range(8):
                    w = []
                    if half == 0 and mc == 0:
                        w = [("s", m_valfm), ("v", p["mV3"]),
                             ("s", st["mr1"]), ("v", st["m_projv"]),
                             ("s", st["m_pb6_s"]), ("s", st["m_pb7_s"])]
                    K.t("matmul",
                        dict(out=bank[:, :], lhsT=val_fm[:, mc, :],
                             rhs=wo_sb[:, mc, half * 512:(half + 1) * 512],
                             start=(mc == 0), stop=False),
                        waits=w, inc=None)
                K.t("matmul",
                    dict(out=bank[:, :], lhsT=identb[:, :],
                         rhs=x_res[:, c % 5, half * 512:(half + 1) * 512],
                         start=False, stop=True),
                    waits=[("v", st["m_xb"][c % 5])] if half == 0 else [],
                    inc=("t" if half == 1 else None))
            st["mWo"] = K.cnt("t")
            for half in range(2):
                K.s("copy",
                    dict(out=r1_sb[:, half * 512:(half + 1) * 512],
                         in_=pb[6 + half][:, :]),
                    waits=([("t", st["mWo"]), ("s", st["mnorm"]),
                            ("v", st["m_ln1red"])] if half == 0 else []),
                    inc="s", drain=True)
                if half == 0:
                    st["mr1h"] = K.cnt("s")
            st["mr1"] = K.cnt("s")
            st["xres_free"][c % 5] = ("s", st["mr1"])
            st["m_ln1red"] = K.v("reduce_sum",
                                 dict(out=stats[:, 0:1], in_=r1_sb[:, :],
                                      axis=AXX, negate=True),
                                 waits=[("s", st["mr1"])], inc="v",
                                 drain=True)
            msq = K.s("activation", dict(out=sq_sb[:, :], in_=r1_sb[:, :],
                                         func=AF.Square),
                      waits=[("s", st["mr1"]), ("v", st["m_sqread"])],
                      inc="s", drain=True)
            st["m_sqread"] = K.v("reduce_sum",
                                 dict(out=stats[:, 2:3], in_=sq_sb[:, :],
                                      axis=AXX),
                                 waits=[("s", msq)], inc="v", drain=True)
            K.v("scalar_tensor_tensor",
                dict(out=stats[:, 4:5], in0=stats[:, 0:1],
                     scalar=1.0 / (D * D), in1=stats[:, 0:1],
                     op0=ALU.mult, op1=ALU.mult), drain=True)
            mvar = K.v("scalar_tensor_tensor",
                       dict(out=stats[:, 5:6], in0=stats[:, 2:3],
                            scalar=1.0 / D, in1=stats[:, 4:5],
                            op0=ALU.mult, op1=ALU.subtract),
                       inc="v", drain=True)
            K.s("activation", dict(out=stats[:, 6:7], in_=stats[:, 5:6],
                                   func=AF.Ln, bias=1e-5),
                waits=[("v", mvar)], drain=True)
            mrstd = K.s("activation", dict(out=stats[:, 7:8],
                                           in_=stats[:, 6:7], func=AF.Exp,
                                           scale=-0.5), inc="s", drain=True)
            mbias = K.v("scalar_tensor_tensor",
                        dict(out=statsb[:, 0:1], in0=stats[:, 0:1],
                             scalar=1.0 / D, in1=stats[:, 7:8],
                             op0=ALU.mult, op1=ALU.mult),
                        waits=[("s", mrstd)], inc="v", drain=True)
            st["mnorm"] = K.s("activation",
                              dict(out=x1_tmb[:, :], in_=r1_sb[:, :],
                                   func=AF.Identity, bias=statsb[:, 0:1],
                                   scale=stats[:, 7:8]),
                              waits=[("v", mbias),
                                     ("t", st["m_x1t_mm"])],
                              inc="s", drain=True)
            p["mnorm"] = st["mnorm"]

        def stage3(c):
            B = pbase(c)
            p = pend.pop(c)
            mx1 = {}
            for dt in range(8):
                bank = pb[B + dt // 4][:, (dt % 4) * 128:(dt % 4) * 128 + 128]
                w = [("s", p["mnorm"]), ("s", p["m_valfm"]),
                     ("s", st["m_xfm_last"])]
                mm = K.t("matmul",
                         dict(out=bank,
                              lhsT=x1_tmb[:, dt * 128:(dt + 1) * 128],
                              rhs=identb[:, :], start=True, stop=True),
                         waits=w, inc=("t" if dt % 4 == 3 else None))
                if dt % 4 == 3:
                    mx1[dt // 4] = mm
            st["m_x1t_mm"] = K.cnt("t")
            for bk2 in range(4):
                K.s("copy",
                    dict(out=x1_fm[:, 2 * bk2:2 * bk2 + 2,
                                   c * 128:(c + 1) * 128],
                         in_=pb[B + bk2 // 2][:, (bk2 % 2) * 256:
                                              (bk2 % 2) * 256 + 256].rearrange(
                             "p (a m) -> p a m", m=128)),
                    waits=[("t", mx1[bk2 // 2]), ("t", st["m_x1fm_free"])],
                    inc="s")
                st["x1e4"][c % 2][bk2] = K.cnt("s")
            st["x1e_par"][c % 2] = K.cnt("s")
            st["mx1evac"] = K.cnt("s")

        def proj(g):
            st["mqk_last_read"] = K.cnt("t")
            m_xev = {}
            xb_tm = {}
            for cc in range(CPG):
                c = CPG * g + cc
                sl = c % 5
                wsem, wval = st["xres_free"][sl]
                mload = K.g("dma_start",
                            dict(out=x_res[:, sl, :],
                                 in_=xin[s, c * 128:(c + 1) * 128, :]),
                            waits=([(wsem, wval)] if wsem else []),
                            inc=(f"gx{sl}", 16))
                for dt in range(8):
                    bank = pb[2 + dt // 2][:, (dt % 2) * 128:
                                           (dt % 2) * 128 + 128]
                    w = [(f"gx{sl}", mload), ("gm", m_ones),
                         ("s", m_xev.get(dt // 2, 0)),
                         ("s", st["x1e_par"][0]), ("s", st["x1e_par"][1]),
                         ("s", st["m_valfm_last"]), ("s", st["mr1"]),
                         ("s", st["m_ktm_last"]),
                         ("v", st["m_Aevac_last"][0]),
                         ("v", st["m_Aevac_last"][1])]
                    mT = K.t("matmul",
                             dict(out=bank,
                                  lhsT=x_res[:, sl, dt * 128:(dt + 1) * 128],
                                  rhs=identb[:, :], start=True, stop=True),
                             waits=w, inc=("t" if dt % 2 == 1 else None))
                    if dt % 2 == 1:
                        m_xev[("t", dt // 2)] = mT
                xb_tm[cc] = K.cnt("t")
                for bk2 in range(4):
                    m_xev[bk2] = K.s(
                        "copy",
                        dict(out=x_fm[:, 2 * bk2:2 * bk2 + 2,
                                      cc * 128:(cc + 1) * 128],
                             in_=pb[2 + bk2][:, 0:256].rearrange(
                                 "p (a m) -> p a m", m=128)),
                        waits=[("t", m_xev[("t", bk2)])], inc="s")
            m_xfm = K.cnt("s")
            st["m_xfm_last"] = m_xfm
            for wsb, bias_sb, dest, wsem in (
                    (wq_sb, bq_sb, q_fm, ("ga", st["attnw_mark"])),
                    (wk_sb, bk_sb, k_fm, ("gb", st["attnw_kv"]))):
                for dt in range(8):
                    bank = pb[6 + dt % 2]
                    for kc in range(8):
                        w = []
                        if kc == 0:
                            w = [("s", m_xfm), wsem,
                                 ("s", st["mr1"]), ("v", st["mV3_last"]),
                                 ("s", st["m_pb6_s" if dt % 2 == 0
                                          else "m_pb7_s"])]
                        mm = K.t("matmul",
                                 dict(out=bank[:, :],
                                      lhsT=wsb[:, kc, dt * 128:(dt + 1) * 128],
                                      rhs=x_fm[:, kc, :],
                                      start=(kc == 0), stop=(kc == 7)),
                                 waits=w, inc=("t" if kc == 7 else None))
                    K.s("activation",
                        dict(out=p_sb[:, :], in_=bank[:, :], func=AF.Relu,
                             bias=bias_sb[:, dt:dt + 1]),
                        waits=[("t", mm), ("v", st["m_tt"])], drain=True)
                    me = K.s("activation",
                             dict(out=en_sb[:, :], in_=bank[:, :],
                                  func=AF.Exp,
                                  bias=bias_sb[:, dt:dt + 1]),
                             inc="s", drain=True)
                    if dt % 2 == 0:
                        st["m_pb6_s"] = me
                    else:
                        st["m_pb7_s"] = me
                    st["m_tt"] = K.v("scalar_tensor_tensor",
                                     dict(out=dest[:, dt, :], in0=en_sb[:, :],
                                          scalar=1.0, in1=p_sb[:, :],
                                          op0=ALU.min, op1=ALU.add),
                                     waits=[("s", me),
                                            ("t", st["mqk_last_read"])],
                                     inc="v", drain=True)
            vpb = {0: 0, 1: 0}
            for cc in range(CPG):
                for half in range(2):
                    bank = pb[6 + half]
                    for kc in range(7):
                        w = []
                        if kc == 0:
                            w = [("s", st["m_pb6_s" if half == 0
                                         else "m_pb7_s"]),
                                 ("v", vpb[half])]
                        K.t("matmul",
                            dict(out=bank[:, :],
                                 lhsT=x_fm[:, kc, cc * 128:(cc + 1) * 128],
                                 rhs=wv_sb[:, kc, half * 512:(half + 1) * 512],
                                 start=(kc == 0), stop=False),
                            waits=w)
                    mm = K.t("matmul",
                             dict(out=bank[:, :],
                                  lhsT=x_fm[:, 7, cc * 128:(cc + 1) * 128],
                                  rhs=wv_sb[:, 7, half * 512:(half + 1) * 512],
                                  start=False, stop=True), inc="t")
                    dst = v_tm[:, cc, half * 520:(half + 1) * 520].rearrange(
                        "p (h m) -> p h m", m=65)[:, :, 0:64]
                    src = bank[:, :].rearrange("p (h m) -> p h m", m=64)
                    bsrc = bvb[:, half * 512:(half + 1) * 512].rearrange(
                        "p (h m) -> p h m", m=64)
                    vpb[half] = K.v("tensor_tensor",
                                    dict(out=dst, in0=src, in1=bsrc,
                                         op=ALU.add),
                                    waits=[("t", mm), ("t", st["mstate"]),
                                           ("s", st["m_biasb"])], inc="v")
            st["m_projv"] = K.cnt("v")
            for cc in range(CPG):
                c = CPG * g + cc
                sl = c % 5
                K.v("tensor_tensor",
                    dict(out=x_res[:, sl, :], in0=x_res[:, sl, :],
                         in1=bob[:, :], op=ALU.add),
                    waits=[("t", xb_tm[cc]), ("s", st["m_biasb"])],
                    inc="v", drain=True)
                st["m_xb"][sl] = K.cnt("v")
            if g == NGRP - 1:
                mfree = K.cnt("t")
                st["arena_free"][0] = mfree
                st["arena_free"][1] = mfree
                st["arena_free"][3] = mfree

        for g in range(NGRP):
            for cc in range(CPG):
                c = CPG * g + cc
                if c - 2 >= 0:
                    stage3(c - 2)
                if cc == 0:
                    proj(g)
                stage1_A(c)
                if c - 1 >= 0:
                    stage1_scast(c - 1)
                    stage2a(c - 1)
                stage1_kT(c)
                if c - 1 >= 0:
                    stage2b(c - 1)
                stage1_rest(c)
        stage3(NCHUNK - 2)
        stage1_scast(NCHUNK - 1)
        stage2a(NCHUNK - 1)
        stage2b(NCHUNK - 1)
        stage3(NCHUNK - 1)
        st["mqk_last_read"] = K.cnt("t")
        st["arena_free"][2] = K.cnt("t")
        st["arena_free"][4] = K.cnt("t")

    def ffn(s):
        K.barrier()
        e4 = g4[:, 0:32]
        e43 = e4.rearrange("p (c e) -> p c e", e=8)
        eqa = g4[:, 32:64]
        eqa3 = eqa.rearrange("p (c e) -> p c e", e=8)
        emb = g4[:, 64:96]
        emb3 = emb.rearrange("p (c e) -> p c e", e=8)
        gsum = g4[:, 96:100]
        gm1 = g4[:, 100:104]
        gm2 = g4[:, 104:108]
        ginv = g4[:, 108:112]
        gw1 = g4[:, 112:116]
        gw2 = g4[:, 116:120]
        gdw = g4[:, 120:124]
        m_gate_ps = 0
        for q4 in range(NCHUNK // 4):
            for j4 in range(4):
                c = 4 * q4 + j4
                for kc in range(8):
                    w = []
                    if kc == 0:
                        w = [("s", m_gate_ps), ("s", st["mx1evac"]),
                             ("gc", K.cnt("gc"))]
                    mm = K.t("matmul",
                             dict(out=pb[6][:, j4 * 8:j4 * 8 + 8],
                                  lhsT=x1_fm[:, kc, c * 128:(c + 1) * 128],
                                  rhs=gwt_sb[:, kc, :],
                                  start=(kc == 0), stop=(kc == 7)),
                             waits=w, inc=("t" if kc == 7 else None))
                K.dummy()
            me = K.s("activation", dict(out=e4, in_=pb[6][:, 0:32],
                                        func=AF.Exp),
                     waits=[("t", mm), ("v", st["m_gate_v"])],
                     inc="s", drain=True)
            m_gate_ps = me
            K.v("reduce_sum", dict(out=gsum, in_=e43, axis=AXX),
                waits=[("s", me)])
            K.v("reduce_max", dict(out=gm1, in_=e43, axis=AXX), drain=True)
            K.v("tensor_tensor",
                dict(out=eqa3, in0=e43,
                     in1=gm1.unsqueeze(-1).broadcast_to([128, 4, 8]),
                     op=ALU.is_ge), drain=True)
            K.v("scalar_tensor_tensor",
                dict(out=emb, in0=eqa, scalar=-1e30, in1=e4,
                     op0=ALU.mult, op1=ALU.add), drain=True)
            K.v("reduce_max", dict(out=gm2, in_=emb3, axis=AXX), drain=True)
            K.v("reciprocal", dict(out=ginv, in_=gsum), drain=True)
            K.v("tensor_tensor", dict(out=gw1, in0=gm1, in1=ginv,
                                      op=ALU.mult), drain=True)
            K.v("tensor_tensor", dict(out=gw2, in0=gm2, in1=ginv,
                                      op=ALU.mult), drain=True)
            K.v("scalar_tensor_tensor",
                dict(out=gdw, in0=gw2, scalar=1.0, in1=gw1,
                     op0=ALU.mult, op1=ALU.add), drain=True)
            K.v("tensor_single_scalar", dict(out=gdw, in_=gdw, scalar=1e-8,
                                             op=ALU.add), drain=True)
            K.v("reciprocal", dict(out=gdw, in_=gdw), drain=True)
            K.v("tensor_tensor", dict(out=gw1, in0=gw1, in1=gdw,
                                      op=ALU.mult), drain=True)
            K.v("tensor_tensor", dict(out=gw2, in0=gw2, in1=gdw,
                                      op=ALU.mult), drain=True)
            K.v("tensor_tensor",
                dict(out=emb3, in0=emb3,
                     in1=gm2.unsqueeze(-1).broadcast_to([128, 4, 8]),
                     op=ALU.is_ge), drain=True)
            K.v("tensor_tensor",
                dict(out=cw_all[:, 4 * q4:4 * q4 + 4, :], in0=eqa3,
                     in1=gw1.unsqueeze(-1).broadcast_to([128, 4, 8]),
                     op=ALU.mult),
                waits=[("t", st["m_cwfm_free"])], drain=True)
            K.v("tensor_tensor",
                dict(out=emb3, in0=emb3,
                     in1=gw2.unsqueeze(-1).broadcast_to([128, 4, 8]),
                     op=ALU.mult), drain=True)
            st["m_gate_v"] = K.v("tensor_tensor",
                                 dict(out=cw_all[:, 4 * q4:4 * q4 + 4, :],
                                      in0=cw_all[:, 4 * q4:4 * q4 + 4, :],
                                      in1=emb3, op=ALU.add),
                                 inc="v", drain=True)
        m_gating_done = K.cnt("v")


        def ln2_chunk(c, wextra, pi=0):
            pair = (pb[6], pb[7]) if pi == 0 else (pb[4], pb[5])
            for dtile in range(8):
                bank = pair[dtile // 4][:, (dtile % 4) * 128:
                                        (dtile % 4) * 128 + 128]
                w = []
                if dtile == 0:
                    w = wextra + [("s", st["m_ln2_s"][pi]),
                                  ("v", st["m_ln2_v"][pi])]
                    if pi == 0:
                        w += [("s", st["m_pb6_s"]), ("s", st["m_pb7_s"])]
                mT = K.t("matmul",
                         dict(out=bank,
                              lhsT=ACC[:, dtile, c * 128:(c + 1) * 128],
                              rhs=identf[:, :], start=True, stop=True),
                         waits=w, inc=("t" if dtile == 7 else None))
            K.v("reduce_sum", dict(out=stats[:, 0:1], in_=pair[0][:, :],
                                   axis=AXX, negate=True), waits=[("t", mT)])
            mred = K.v("reduce_sum", dict(out=stats[:, 1:2], in_=pair[1][:, :],
                                          axis=AXX, negate=True),
                       inc="v", drain=True)
            st["m_ln2_v"][pi] = mred
            K.v("tensor_tensor", dict(out=stats[:, 2:3], in0=stats[:, 0:1],
                                      in1=stats[:, 1:2], op=ALU.add),
                drain=True)
            K.s("activation", dict(out=sq_sb[:, 0:512], in_=pair[0][:, :],
                                   func=AF.Square),
                waits=[("t", mT), ("v", mred), ("v", st["m_ln2sq_v"])],
                drain=True)
            ms2 = K.s("activation", dict(out=sq_sb[:, 512:1024],
                                         in_=pair[1][:, :], func=AF.Square),
                      inc="s", drain=True)
            mrs = K.v("reduce_sum", dict(out=stats[:, 4:5], in_=sq_sb[:, :],
                                         axis=AXX),
                      waits=[("s", ms2)], inc="v", drain=True)
            st["m_ln2sq_v"] = mrs
            K.v("scalar_tensor_tensor",
                dict(out=stats[:, 6:7], in0=stats[:, 2:3],
                     scalar=1.0 / (D * D), in1=stats[:, 2:3],
                     op0=ALU.mult, op1=ALU.mult), drain=True)
            mvar = K.v("scalar_tensor_tensor",
                       dict(out=stats[:, 7:8], in0=stats[:, 4:5],
                            scalar=1.0 / D, in1=stats[:, 6:7],
                            op0=ALU.mult, op1=ALU.subtract),
                       inc="v", drain=True)
            K.s("activation", dict(out=stats[:, 8:9], in_=stats[:, 7:8],
                                   func=AF.Ln, bias=1e-5),
                waits=[("v", mvar)], drain=True)
            mrstd = K.s("activation", dict(out=stats[:, 9:10],
                                           in_=stats[:, 8:9],
                                           func=AF.Exp, scale=-0.5),
                        inc="s", drain=True)
            mbias = K.v("scalar_tensor_tensor",
                        dict(out=statsb[:, 1:2], in0=stats[:, 2:3],
                             scalar=1.0 / D, in1=stats[:, 9:10],
                             op0=ALU.mult, op1=ALU.mult),
                        waits=[("s", mrstd)], inc="v", drain=True)
            hsem = ("yo", "yp")
            for half in range(2):
                K.s("activation",
                    dict(out=out_sb[:, half * 512:(half + 1) * 512],
                         in_=pair[half][:, :], func=AF.Identity,
                         bias=statsb[:, 1:2], scale=stats[:, 9:10]),
                    waits=[("v", mbias),
                           (hsem[half], st["m_out_half"][half])],
                    inc=("s" if half == 1 else None), drain=True)
            mnorm = K.cnt("s")
            st["m_ln2_s"][pi] = mnorm
            if pi == 0:
                st["m_pb6_s"] = mnorm
                st["m_pb7_s"] = mnorm
            for half in range(2):
                st["m_out_half"][half] = K.y(
                    "dma_start",
                    dict(out=out[s, c * 128:(c + 1) * 128,
                                 half * 512:(half + 1) * 512],
                         in_=out_sb[:, half * 512:(half + 1) * 512]),
                    waits=([("s", mnorm)] if half == 0 else []),
                    inc=(hsem[half], 16))

        def emit_cwt(e2, cb2):
            mts4 = []
            for j in range(4):
                c4 = cb2 * 4 + j
                mts4.append(K.v(
                    "tensor_single_scalar",
                    dict(out=cwt_sb[:, j, :], in_=onespk[:, :],
                         scalar=cw_all[:, c4, e2 - 1:e2],
                         op=ALU.mult),
                    waits=([("t", st["m_cwt_mm"])] if j == 0 else []),
                    inc="v"))
            return mts4

        pend_ln2 = []
        m_acc_v = {}
        m_act_read = 0
        m_sg = [0, 0]
        m_actf = [0, 0]
        m_act_ft = [0] * NFT
        m_uu = [0, 0]
        m_cwb = [0, 0]
        mcb = 0
        for e in range(NUNIT):
            sgu = gu_sl[(2 * e) % 3]
            suu = gu_sl[(2 * e + 1) % 3]
            sdt = dt_sl[e % 2]
            for cb in range(NCB):
                if e > 0:
                    slot = cb % 2
                    mts4 = st["mts4"]
                    if mts4 is None:
                        mts4 = emit_cwt(e, cb)
                    st["mts4"] = None
                    for j in range(4):
                        K.t("matmul",
                            dict(out=pb[6][:, j * 128:(j + 1) * 128],
                                 lhsT=cwt_sb[:, j, :],
                                 rhs=identb[:, :], start=True, stop=True),
                            waits=[("v", mts4[j])]
                            + ([("s", st["m_pb6_s"])] if j == 0 else []),
                            inc=("t" if j == 3 else None))
                    st["m_cwt_mm"] = K.cnt("t")
                    mcb = K.s("copy", dict(out=cwb_sb[:, slot, :],
                                           in_=pb[6][:, :]),
                              waits=[("t", st["m_cwt_mm"])], inc="s")
                    st["m_pb6_s"] = mcb
                for ft in range(NFT):
                    wf = FT_W[ft]
                    bg = pb[0 + (ft % 2) * 2]
                    bu = pb[1 + (ft % 2) * 2]
                    for kc in range(8):
                        w = []
                        if kc == 0:
                            w = [st["ffnw_g"][e], ("s", m_sg[ft % 2])]
                        K.t("matmul",
                            dict(out=bg[0:wf, :],
                                 lhsT=sgu[:, kc, ft * 128:ft * 128 + wf],
                                 rhs=x1_fm[:, kc, cb * CB:(cb + 1) * CB],
                                 start=(kc == 0), stop=(kc == 7)),
                            waits=w, inc=("t" if kc == 7 else None))
                    mg = K.cnt("t")
                    for kc in range(8):
                        w = ([("v", m_uu[ft % 2]), st["ffnw_mark"][e]]
                             if kc == 0 else [])
                        K.t("matmul",
                            dict(out=bu[0:wf, :],
                                 lhsT=suu[:, kc, ft * 128:ft * 128 + wf],
                                 rhs=x1_fm[:, kc, cb * CB:(cb + 1) * CB],
                                 start=(kc == 0), stop=(kc == 7)),
                            waits=w, inc=("t" if kc == 7 else None))
                    mu = K.cnt("t")
                    msg = K.s("activation",
                              dict(out=sg_sb[0:wf, ft % 2, :], in_=bg[0:wf, :],
                                   func=AF.Silu),
                              waits=[("t", mg), ("v", m_actf[ft % 2])], inc="s")
                    m_sg[ft % 2] = msg
                    if e > 0:
                        muu = K.v("tensor_tensor",
                                  dict(out=uu_sb[0:wf, ft % 2, :],
                                       in0=bu[0:wf, :],
                                       in1=cwb_sb[0:wf, cb % 2, :],
                                       op=ALU.mult),
                                  waits=[("t", mu), ("s", mcb)], inc="v",
                                  drain=True)
                    else:
                        muu = K.v("tensor_copy",
                                  dict(out=uu_sb[0:wf, ft % 2, :],
                                       in_=bu[0:wf, :]),
                                  waits=[("t", mu)], inc="v", drain=True)
                    m_uu[ft % 2] = muu
                    if cb == NCB - 1 and ft == NFT - 1:
                        st["arena_free"][(2 * e) % 3] = K.cnt("t")
                        st["arena_free"][(2 * e + 1) % 3] = K.cnt("t")
                    m_actf[ft % 2] = K.v("tensor_tensor",
                        dict(out=act_sb[0:wf, ft, :],
                             in0=sg_sb[0:wf, ft % 2, :],
                             in1=uu_sb[0:wf, ft % 2, :], op=ALU.mult),
                        waits=[("s", msg), ("t", m_act_read)],
                        inc="v", drain=True)
                    m_act_ft[ft] = m_actf[ft % 2]
                    if ft % 2 == 1 and pend_ln2:
                        c_, wx_ = pend_ln2.pop(0)
                        ln2_chunk(c_, wx_, pi=0)
                m_act = K.cnt("v")
                if e > 0:
                    m_cwb[cb % 2] = m_act
                e2, cb2 = (e, cb + 1) if cb + 1 < NCB else (e + 1, 0)
                if e2 < NUNIT and e2 > 0:
                    st["mts4"] = emit_cwt(e2, cb2)
                for dm in range(8):
                    bank = pb[4 + dm % 2]
                    for ic in range(8):
                        wic = FT_W[ic]
                        w = [("v", m_act_ft[ic])]
                        if ic == 0:
                            w.append(("v", m_acc_v.get(dm % 2, 0)))
                            if e == NUNIT - 1 and dm < 2:
                                w += [("s", st["m_ln2_s"][1]),
                                      ("v", st["m_ln2_v"][1])]
                        K.t("matmul",
                            dict(out=bank[:, :],
                                 lhsT=sdt[0:wic, ic, dm * 128:(dm + 1) * 128],
                                 rhs=act_sb[0:wic, ic, :],
                                 start=(ic == 0), stop=(ic == 7)),
                            waits=w, inc=("t" if ic == 7 else None))
                    mdn = K.cnt("t")
                    in0 = (x1_fm[:, dm, cb * CB:(cb + 1) * CB] if e == 0
                           else ACC[:, dm, cb * CB:(cb + 1) * CB])
                    mad = K.v("tensor_tensor",
                              dict(out=ACC[:, dm, cb * CB:(cb + 1) * CB],
                                   in0=in0, in1=bank[:, :], op=ALU.add),
                              waits=[("t", mdn)],
                              inc="v", drain=True)
                    m_acc_v[dm % 2] = mad
                m_act_read = K.cnt("t")
                if cb == NCB - 1:
                    st["arena_free"][3 + e % 2] = K.cnt("t")
                if e == NUNIT - 1:
                    for j in range(4):
                        pend_ln2.append(
                            (cb * 4 + j,
                             ([("v", m_acc_v[0]), ("v", m_acc_v[1])]
                              if j == 0 else [])))
                    if cb == NCB - 1:
                        for j in range(len(pend_ln2)):
                            c_, wx_ = pend_ln2[j]
                            ln2_chunk(c_, wx_, pi=c_ % 2)
                        pend_ln2.clear()
            if e + 1 < NUNIT:
                load_unit_weights(e + 1, gen[0])
                gen[0] += 1
        m_units_t = K.cnt("t")
        m_units_v = K.cnt("v")
        st["m_x1fm_free"] = m_units_t
        st["m_cwfm_free"] = m_units_t

        if s == 0 and STAGE == "full":
            load_attn_weights(first=False)

    gen = [0]
    build_bias_tiles()
    load_attn_weights(first=True)
    phase_A(0)
    if STAGE != "a1":
        load_unit_weights(0, gen[0]); gen[0] += 1
        ffn(0)
        if STAGE == "full":
            phase_A(1)
            load_unit_weights(0, gen[0]); gen[0] += 1
            ffn(1)
    K.y("nop", dict(), waits=[("yo", K.cnt("yo")), ("yp", K.cnt("yp"))])
    K.check()

    sems_names = ["t", "v", "s", "gm", "gc", "ga", "gf0", "gf1",
                  "gf2", "yc", "yx0", "yx1", "yo", "gx0", "gx1", "gx2", "gx3", "gx4",
                  "gh0", "gh1", "gh2", "yp", "gb"]
    stack = contextlib.ExitStack()
    sems = {n: stack.enter_context(nc.semaphore(f"sem_{n}")) for n in sems_names}
    _replay_all(nc, K, sems)
    stack.close()
    return nc



def _prep_weights(inputs):
    f32 = np.float32
    GU = np.asarray(inputs["L0_ffn_shared_gate_up"], f32)
    EGU = np.asarray(inputs["L0_ffn_experts_gate_up"], f32)
    DN = np.asarray(inputs["L0_ffn_shared_down"], f32)
    EDN = np.asarray(inputs["L0_ffn_experts_down"], f32)
    gug = np.empty((NUNIT, D, I_FF), f32)
    guu = np.empty((NUNIT, D, I_FF), f32)
    dnt = np.empty((NUNIT, I_FF, D), f32)
    gug[0] = GU[:I_FF].T
    guu[0] = GU[I_FF:].T
    dnt[0] = DN.T
    for e in range(8):
        gug[e + 1] = EGU[e, :I_FF].T
        guu[e + 1] = EGU[e, I_FF:].T
        dnt[e + 1] = EDN[e].T
    def pt(w):
        return np.ascontiguousarray(
            np.asarray(w, f32).reshape(8, 128, D).transpose(1, 0, 2))

    return {
        "wq": pt(inputs["L0_attn_Wq"]),
        "wk": pt(inputs["L0_attn_Wk"]),
        "wv": pt(inputs["L0_attn_Wv"]),
        "wo": pt(inputs["L0_attn_Wo"]),
        "bq": np.ascontiguousarray(np.asarray(inputs["L0_attn_bq"], f32)),
        "bk": np.ascontiguousarray(np.asarray(inputs["L0_attn_bk"], f32)),
        "bv": np.ascontiguousarray(np.asarray(inputs["L0_attn_bv"], f32)),
        "bo": np.ascontiguousarray(np.asarray(inputs["L0_attn_bo"], f32)),
        "gwt": np.ascontiguousarray(np.asarray(inputs["L0_ffn_gate_weight"],
                                               f32).T),
        "gug": np.ascontiguousarray(
            gug.reshape(NUNIT, 8, 128, I_FF).transpose(0, 2, 1, 3)),
        "guu": np.ascontiguousarray(
            guu.reshape(NUNIT, 8, 128, I_FF).transpose(0, 2, 1, 3)),
        "dnt": np.ascontiguousarray(
            dnt[:, :896].reshape(NUNIT, 7, 128, D).transpose(0, 2, 1, 3)),
        "dntt": np.ascontiguousarray(dnt[:, 896:938]),
        "cmask": np.ascontiguousarray(
            (np.arange(128)[:, None] <= np.arange(128)[None, :]).astype(f32)),
        "identf": np.eye(128, dtype=f32),
        "identb": np.eye(128, dtype=f32).astype(ml_dtypes.bfloat16),
        "onespk": np.ones((128, 128), ml_dtypes.bfloat16),
    }


def kernel(**inputs):
    x = np.asarray(inputs["L0_attn_x"], np.float32)
    common = _prep_weights(inputs)
    nc = build_nc()
    in_maps = []
    for c in range(NCORE):
        m = dict(common)
        m["xin"] = np.ascontiguousarray(x[2 * c:2 * c + 2])
        in_maps.append(m)
    res = run_bass_kernel_spmd(nc, in_maps, core_ids=list(range(NCORE)))
    outs = [np.asarray(res.results[c]["out"]).reshape(2, SEQ, D)
            for c in range(NCORE)]
    return np.concatenate(outs, axis=0)
